# version 10
# baseline (speedup 1.0000x reference)
import numpy as np
import ml_dtypes
from contextlib import ExitStack

import concourse.bass as bass
import concourse.mybir as mybir
from concourse.bass_utils import run_bass_kernel_spmd

F32 = mybir.dt.float32
BF16 = mybir.dt.bfloat16
AF = mybir.ActivationFunctionType
ALU = mybir.AluOpType
BF = ml_dtypes.bfloat16

T, D = 4096, 1024
NCORES = 8


class Buf:
    __slots__ = ("name", "h", "last_writer", "readers")

    def __init__(self, name, h):
        self.name = name
        self.h = h
        self.last_writer = None
        self.readers = []

    def __getitem__(self, sl):
        return self.h[sl]

    @property
    def ap(self):
        return self.h[:]


class _Op:
    __slots__ = ("idx", "eng", "fn", "deps", "waw", "need_inc", "is_dma",
                 "inc_val", "engine_pos", "tag")


ENGINES = ("pe", "act", "dve", "pool", "sp")
RPOOL = {"sp": 20, "pool": 8, "act": 8, "dve": 4, "pe": 4}
BLOCK_ATTR = {"pe": "tensor", "act": "scalar", "dve": "vector",
              "pool": "gpsimd", "sp": "sync"}


class Prog:
    def __init__(self, nc):
        self.nc = nc
        self.ops = []
        self.stack = ExitStack()
        self._scopes = []
        self.sbuf_bytes = 0
        self._names = set()

    def push_scope(self):
        self._scopes.append(ExitStack())

    def pop_scope(self):
        self._scopes.pop().close()

    @property
    def _alloc_stack(self):
        return self._scopes[-1] if self._scopes else self.stack

    def _uname(self, name):
        i = 0
        n = name
        while n in self._names:
            i += 1
            n = f"{name}_{i}"
        self._names.add(n)
        return n

    def sbuf(self, name, shape, dtype):
        name = self._uname(name)
        h = self._alloc_stack.enter_context(
            self.nc.sbuf_tensor(name, shape, dtype))
        esz = 4 if dtype == F32 else 2
        self.sbuf_bytes += shape[1] * esz
        return Buf(name, h)

    def psum(self, name, shape, dtype):
        name = self._uname(name)
        h = self._alloc_stack.enter_context(
            self.nc.psum_tensor(name, shape, dtype))
        return Buf(name, h)

    def dram(self, name, shape, dtype, kind="Internal"):
        name = self._uname(name)
        h = self.nc.dram_tensor(name, shape, dtype, kind=kind)
        return Buf(name, h)

    def op(self, eng, fn, reads=(), writes=(), dma=False, extra_deps=(), tag=""):
        i = len(self.ops)
        o = _Op()
        o.idx, o.eng, o.fn, o.is_dma, o.tag = i, eng, fn, dma, tag
        o.need_inc = dma
        deps = set(extra_deps)
        waw = set()
        for b in reads:
            if b.last_writer is not None:
                deps.add(b.last_writer)
            b.readers.append(i)
        for b in writes:
            if b.last_writer is not None:
                waw.add(b.last_writer)
            deps.update(b.readers)
            b.last_writer = i
            b.readers = []
        waw -= deps
        deps.discard(i)
        waw.discard(i)
        o.deps = deps
        o.waw = waw
        self.ops.append(o)
        return i

    def barrier(self):
        last_c = {}
        last_d = {e: [] for e in ENGINES}
        for o in self.ops:
            if o.is_dma:
                last_d[o.eng].append(o.idx)
            else:
                last_c[o.eng] = o.idx
        deps = set(last_c.values())
        for e, lst in last_d.items():
            deps.update(lst[-RPOOL[e]:])
        used = sorted({o.eng for o in self.ops})

        def f(eng):
            eng.drain()
            return eng.nop()
        firsts = [self.op(e, f, extra_deps=tuple(deps), tag="barrier1")
                  for e in used]
        for e in used:
            self.op(e, lambda eng: eng.nop(), extra_deps=tuple(firsts),
                    tag="barrier2")

    def emit(self, verbose=False, paranoid=False):
        ops = self.ops
        if paranoid:
            for i, o in enumerate(ops):
                o.deps = set(range(i))
        for o in ops:
            for d in o.deps | o.waw:
                p = ops[d]
                if p.is_dma or p.eng != o.eng:
                    p.need_inc = True
        cnt_c = {e: 0 for e in ENGINES}
        cnt_d = {e: 0 for e in ENGINES}
        pos = {e: 0 for e in ENGINES}
        for o in ops:
            o.engine_pos = pos[o.eng]
            pos[o.eng] += 1
            if o.need_inc:
                if o.is_dma:
                    o.inc_val = cnt_d[o.eng]
                    cnt_d[o.eng] += 1
                else:
                    cnt_c[o.eng] += 1
                    o.inc_val = cnt_c[o.eng]
        used = sorted({o.eng for o in ops})
        sems = {}
        for e in used:
            if any(o.need_inc and not o.is_dma for o in ops if o.eng == e):
                sems[("c", e)] = self.stack.enter_context(
                    self.nc.semaphore(f"sem_c_{e}"))
            nd = cnt_d[e]
            if nd:
                for r in range(min(RPOOL[e], nd)):
                    sems[("d", e, r)] = self.stack.enter_context(
                        self.nc.semaphore(f"sem_d_{e}_{r}"))
        watermark = {e: {} for e in ENGINES}
        last_drain = {e: -1 for e in ENGINES}
        plans = []
        n_waits = 0
        n_drains = 0
        for o in ops:
            waits = []
            pre_drain = False
            gate = None
            if o.is_dma:
                R = RPOOL[o.eng]
                i = o.inc_val
                if i >= R:
                    gate = (("d", o.eng, i % R), 16 * (i // R))
            for d in sorted(o.deps | o.waw):
                p = ops[d]
                if (not p.is_dma) and p.eng == o.eng:
                    if o.eng in ("pe", "sp"):
                        continue
                    if d in o.waw:
                        continue
                    if p.engine_pos > last_drain[o.eng]:
                        pre_drain = True
                elif p.is_dma:
                    R = RPOOL[p.eng]
                    key = ("d", p.eng, p.inc_val % R)
                    v = 16 * (p.inc_val // R + 1)
                    if watermark[o.eng].get(key, 0) < v:
                        waits.append((key, v))
                else:
                    key = ("c", p.eng)
                    v = p.inc_val
                    if watermark[o.eng].get(key, 0) < v:
                        waits.append((key, v))
            if pre_drain:
                last_drain[o.eng] = o.engine_pos - 1
                n_drains += 1
            wd = {}
            for k, v in waits:
                wd[k] = max(wd.get(k, 0), v)
            if gate is not None and wd.get(gate[0], 0) < gate[1]:
                wd[gate[0]] = max(wd.get(gate[0], 0), gate[1])
            for k, v in wd.items():
                watermark[o.eng][k] = v
            plans.append((o, sorted(wd.items()), pre_drain))
            n_waits += len(wd)
        if verbose:
            per_eng = {e: sum(1 for o in ops if o.eng == e) for e in used}
            print(f"[prog] ops={len(ops)} per-engine={per_eng} waits={n_waits} "
                  f"drains={n_drains} sbuf_bytes/part={self.sbuf_bytes}")
        final_waits = []
        for e in used:
            nd = cnt_d[e]
            R = RPOOL[e]
            for r in range(min(R, nd)):
                uses = (nd - 1 - r) // R + 1
                final_waits.append((("d", e, r), 16 * uses))
            if ("c", e) in sems and cnt_c[e]:
                final_waits.append((("c", e), cnt_c[e]))

        eng_plans = {e: [] for e in used}
        for pl in plans:
            eng_plans[pl[0].eng].append(pl)

        blk = self.stack.enter_context(self.nc.Block())

        def mk_body(e, my_plans):
            def body(eng):
                for o, waits, pre_drain in my_plans:
                    for k, v in waits:
                        eng.wait_ge(sems[k], v)
                    if pre_drain:
                        eng.drain()
                    instr = o.fn(eng)
                    if o.need_inc:
                        if o.is_dma:
                            key = ("d", o.eng, o.inc_val % RPOOL[o.eng])
                            instr.then_inc(sems[key], 16)
                        else:
                            instr.then_inc(sems[("c", o.eng)], 1)
                return None
            return body

        for e in used:
            if e == "sp":
                continue
            getattr(blk, BLOCK_ATTR[e])(mk_body(e, eng_plans[e]))
        def sp_body(eng):
            for o, waits, pre_drain in eng_plans.get("sp", []):
                for k, v in waits:
                    eng.wait_ge(sems[k], v)
                instr = o.fn(eng)
                if o.need_inc:
                    if o.is_dma:
                        instr.then_inc(sems[("d", "sp", o.inc_val % RPOOL["sp"])], 16)
                    else:
                        instr.then_inc(sems[("c", "sp")], 1)
            for k, v in final_waits:
                eng.wait_ge(sems[k], v)
        blk.sync(sp_body)
        self.stack.close()



def dma(p, q, dst, dst_ap, src, src_ap, tag="", **kw):
    p.op(q, lambda e: e.dma_start(out=dst_ap, in_=src_ap, **kw),
         reads=(src,), writes=(dst,), dma=True, tag=tag)


def dma_tr(p, dst, dst_ap, src, src_ap, tag="", q="sp"):
    p.op(q, lambda e: e.dma_start_transpose(dst_ap, src_ap),
         reads=(src,), writes=(dst,), dma=True, tag=tag)


def mm(p, out, out_ap, lhsT, lhsT_ap, rhs, rhs_ap, start, stop, tag=""):
    p.op("pe", lambda e: e.matmul(out_ap, lhsT_ap, rhs_ap, start=start, stop=stop),
         reads=(lhsT, rhs), writes=(out,), tag=tag)


def transpose(p, out, out_ap, src, src_ap, ident, kdim=128, tag=""):
    id_ap = ident.ap if kdim == 128 else ident.h[0:kdim, 0:kdim]
    p.op("pe", lambda e: e.transpose(out_ap, src_ap, id_ap),
         reads=(src, ident), writes=(out,), tag=tag)


def act(p, out, out_ap, src, src_ap, func, bias=None, scale=None, accum=None,
        accum_ap=None, extra_reads=(), tag=""):
    kw = {}
    if bias is not None:
        kw["bias"] = bias
    if scale is not None:
        kw["scale"] = scale
    if accum is not None:
        kw["accum_out"] = accum_ap
    writes = [out] if out is not None else []
    if accum is not None:
        writes.append(accum)
    p.op("act", lambda e: e.activation(out=out_ap, in_=src_ap, func=func, **kw),
         reads=(src,) + tuple(extra_reads), writes=tuple(writes), tag=tag)
    if accum is not None:
        def _f(e):
            e.drain()
            return e.nop()
        p.op("act", _f, reads=(accum,), writes=(accum,), tag=tag + "_sync")


def tt(p, eng, out, out_ap, a, a_ap, b, b_ap, op, tag=""):
    p.op(eng, lambda e: e.tensor_tensor(out=out_ap, in0=a_ap, in1=b_ap, op=op),
         reads=(a, b), writes=(out,), tag=tag)


def ts(p, eng, out, out_ap, a, a_ap, s1, op0, s2=None, op1=None, extra_reads=(),
       tag=""):
    kw = dict(scalar1=s1, scalar2=s2, op0=op0)
    if op1 is not None:
        kw["op1"] = op1
    p.op(eng, lambda e: e.tensor_scalar(out=out_ap, in0=a_ap, **kw),
         reads=(a,) + tuple(extra_reads), writes=(out,), tag=tag)


def stt(p, eng, out, out_ap, a, a_ap, scalar, b, b_ap, op0, op1,
        extra_reads=(), tag=""):
    p.op(eng, lambda e: e.scalar_tensor_tensor(out=out_ap, in0=a_ap,
                                               scalar=scalar, in1=b_ap,
                                               op0=op0, op1=op1),
         reads=(a, b) + tuple(extra_reads), writes=(out,), tag=tag)


def copy(p, eng, out, out_ap, src, src_ap, tag=""):
    if eng == "act":
        p.op(eng, lambda e: e.copy(out=out_ap, in_=src_ap),
             reads=(src,), writes=(out,), tag=tag)
    else:
        p.op(eng, lambda e: e.tensor_copy(out=out_ap, in_=src_ap),
             reads=(src,), writes=(out,), tag=tag)


def reduce(p, out, out_ap, src, src_ap, op, negate=False, tag=""):
    p.op("dve", lambda e: e.tensor_reduce(out_ap, src_ap,
                                          mybir.AxisListType.X, op,
                                          negate=negate),
         reads=(src,), writes=(out,), tag=tag)


def scan(p, out, out_ap, d0, d0_ap, d1, d1_ap, initial, op0, op1,
         extra_reads=(), tag=""):
    p.op("dve", lambda e: e.tensor_tensor_scan(out=out_ap, data0=d0_ap,
                                               data1=d1_ap, initial=initial,
                                               op0=op0, op1=op1),
         reads=(d0, d1) + tuple(extra_reads), writes=(out,), tag=tag)


def memset(p, eng, buf, ap, val, tag=""):
    p.op(eng, lambda e: e.memset(ap, val), writes=(buf,), tag=tag)


def recip(p, out, out_ap, src, src_ap, tag=""):
    p.op("dve", lambda e: e.reciprocal(out_ap, src_ap),
         reads=(src,), writes=(out,), tag=tag)



def host_constants():
    c = {}
    c["c_ident"] = np.eye(128, dtype=BF)
    c["c_ones1"] = np.ones((1, 128), dtype=BF)
    m = np.zeros((128, 128), np.float32)
    m[np.triu_indices(128, 1)] = -1e9
    c["c_mask"] = m
    i = np.arange(64, dtype=np.float64)
    theta = 10000.0 ** (-i / 32.0)
    pos = np.arange(T, dtype=np.float64)
    ang = (pos[:, None].astype(np.float32).astype(np.float64)
           * theta[None, :].astype(np.float32).astype(np.float64))
    ang = ang.astype(np.float32).astype(np.float64)
    c["c_cos"] = np.cos(ang).astype(np.float32)
    c["c_sin"] = np.sin(ang).astype(np.float32)
    return c


def declare_constants(p):
    nc = p.nc
    g = {}
    for nm, shape, dt in [("c_ident", (128, 128), BF16),
                          ("c_ones1", (1, 128), BF16),
                          ("c_mask", (128, 128), F32),
                          ("c_cos", (T, 64), F32),
                          ("c_sin", (T, 64), F32)]:
        g[nm] = Buf(nm, nc.dram_tensor(nm, list(shape), dt,
                                       kind="ExternalInput"))
    sb = {}
    sb["ident"] = p.sbuf("ident", [128, 128], BF16)
    sb["ones1"] = p.sbuf("ones1", [1, 128], BF16)
    sb["mask"] = p.sbuf("mask", [128, 128], F32)
    dma(p, "sp", sb["ident"], sb["ident"].ap, g["c_ident"], g["c_ident"][:, :])
    dma(p, "sp", sb["ones1"], sb["ones1"].ap, g["c_ones1"], g["c_ones1"][:, :])
    dma(p, "sp", sb["mask"], sb["mask"].ap, g["c_mask"], g["c_mask"][:, :])
    sb["_ccos_d"] = g["c_cos"]
    sb["_csin_d"] = g["c_sin"]
    return sb


SCALE0 = float(128 ** -0.5)


def declare_l0_inputs(nc):
    g = {}
    def di(name, shape):
        g[name] = Buf(name, nc.dram_tensor(name, list(shape), F32,
                                           kind="ExternalInput"))
    di("L0_attn_x", (T, D))
    di("L0_attn_Wq", (2, 128, D))
    di("L0_attn_Wk", (128, D))
    di("L0_attn_Wv", (128, D))
    di("L0_attn_Wo", (D, 256))
    di("L0_ffn_router_w", (8, D))
    di("L0_ffn_w1_w", (8, 4096, D))
    di("L0_ffn_w1_b", (8, 32, 128))
    di("L0_ffn_w2_w", (8, D, 4096))
    di("L0_ffn_w2_b", (8, D))
    di("L0_ffn_wg_w", (8, 4096, D))
    di("L0_ffn_wg_b", (8, 32, 128))
    return g


def build_l0(p, g, cst, x1_out, dbg=(), nw_limit=None):
    nc = p.nc
    W = 512
    NW = T // W
    if nw_limit:
        NW = nw_limit
    NT = 4

    wq_bf = p.dram("wq_bf", [2, 128, D], BF16)
    wk_bf = p.dram("wk_bf", [128, D], BF16)
    wv_bf = p.dram("wv_bf", [128, D], BF16)
    wo_bf = p.dram("wo_bf", [D, 256], BF16)
    wg_bf = p.dram("wg_bf", [8, 4096, D], BF16)
    w1_bf = p.dram("w1_bf", [8, 4096, D], BF16)
    w2_bf = p.dram("w2_bf", [8, D, 4096], BF16)

    MD = dict(max_dma_last_dim=8192)
    for h in range(2):
        dma(p, "pool", wq_bf, wq_bf[h, :, :], g["L0_attn_Wq"],
            g["L0_attn_Wq"][h, :, :], tag="precast", **MD)
    dma(p, "pool", wk_bf, wk_bf[:, :], g["L0_attn_Wk"], g["L0_attn_Wk"][:, :],
        tag="precast", **MD)
    dma(p, "pool", wv_bf, wv_bf[:, :], g["L0_attn_Wv"], g["L0_attn_Wv"][:, :],
        tag="precast", **MD)
    dma(p, "pool", wo_bf, wo_bf[:, :], g["L0_attn_Wo"], g["L0_attn_Wo"][:, :],
        tag="precast", **MD)
    ptr_bf = [p.psum(f"ptr{i}", [128, 1024], BF16) for i in range(2)]
    WqT = p.sbuf("WqT", [128, 2 * 1024], BF16)
    WkT = p.sbuf("WkT", [128, 1024], BF16)
    WvT = p.sbuf("WvT", [128, 1024], BF16)
    WoT = p.sbuf("WoT", [128, 2 * 1024], BF16)
    for h in range(2):
        for dc in range(8):
            dma_tr(p, WqT, WqT[:, (h * 8 + dc) * 128:(h * 8 + dc) * 128 + 128],
                   wq_bf, wq_bf[h, 0:128, 128 * dc:128 * dc + 128])
    for dc in range(8):
        dma_tr(p, WkT, WkT[:, 128 * dc:128 * dc + 128],
               wk_bf, wk_bf[0:128, 128 * dc:128 * dc + 128])
        dma_tr(p, WvT, WvT[:, 128 * dc:128 * dc + 128],
               wv_bf, wv_bf[0:128, 128 * dc:128 * dc + 128])
    for h in range(2):
        dma_tr(p, WoT, WoT[:, 1024 * h:1024 * h + 1024],
               wo_bf, wo_bf[:, 128 * h:128 * h + 128])
    idf0 = p.sbuf("ident_f32_l0", [128, 128], F32)
    copy(p, "dve", idf0, idf0.ap, cst["ident"], cst["ident"].ap)
    cst["ident_f32"] = idf0
    router_sb = p.sbuf("router_sb", [8, 1024], F32)
    dma(p, "sp", router_sb, router_sb.h[0:8, :], g["L0_ffn_router_w"],
        g["L0_ffn_router_w"][:, :])
    routerT = p.sbuf("routerT", [128, 64], F32)
    w1b_cols = p.sbuf("w1b_cols", [128, 8 * 32], BF16)
    wgb_cols = p.sbuf("wgb_cols", [128, 8 * 32], BF16)
    bias_nat = p.sbuf("bias_nat", [32, 128], BF16)
    for e in range(8):
        for (srcb, dstb) in ((g["L0_ffn_w1_b"], w1b_cols),
                             (g["L0_ffn_wg_b"], wgb_cols)):
            dma(p, "pool", bias_nat, bias_nat.h[0:32, :], srcb,
                srcb[e, :, :])
            pt = ptr_bf[0]
            transpose(p, pt, pt.h[:, 0:32], bias_nat,
                      bias_nat.h[0:32, :], cst["ident"], kdim=32)
            copy(p, "dve", dstb, dstb[:, 32 * e:32 * e + 32], pt,
                 pt.h[:, 0:32])
    w2b_row = [p.sbuf(f"w2b_row{i}", [1, 1024], BF16) for i in range(2)]

    xn0_fm = p.sbuf("xn0_fm", [128, 8 * W], BF16)
    acc = p.sbuf("acc_moe", [128, NT * 1024], F32)
    h_buf = p.sbuf("h_buf", [128, 16 * W], BF16)
    wgT_sl = [p.sbuf(f"wgT_sl{i}", [128, 8 * 256], BF16) for i in range(2)]
    w1T_sl = [p.sbuf(f"w1T_sl{i}", [128, 8 * 256], BF16) for i in range(2)]
    w2T_buf = p.sbuf("w2T_buf", [128, 16 * 1024], BF16)
    K_fm = p.sbuf("K_fm", [128, T], BF16)
    V_ext = p.sbuf("V_ext", [128, 32 * 132], BF16)
    memset(p, "dve", V_ext, V_ext.ap, 1.0)
    scores = p.sbuf("scores", [128, T], F32)
    Pb = [p.sbuf(f"Pb{i}", [128, 512], BF16) for i in range(2)]
    PTb = [p.sbuf(f"PTb{i}", [128, 128], BF16) for i in range(2)]
    ropebuf = p.sbuf("ropebuf", [128, 2 * 64], F32)
    ropebuf2 = p.sbuf("ropebuf2", [128, 2 * 64], F32)
    krope = [p.sbuf(f"krope{i}", [128, 128], BF16) for i in range(2)]
    qfm = p.sbuf("qfm", [128, 128], BF16)
    attn_fm = p.sbuf("attn_fm", [128, 256], BF16)
    atm = p.sbuf("atm", [128, 128], BF16)
    costl = p.sbuf("costl", [128, NT * 64], F32)
    sintl = p.sbuf("sintl", [128, NT * 64], F32)
    xtile = [p.sbuf(f"xtile{i}", [128, 1024], F32) for i in range(2)]
    junk = p.sbuf("junk", [128, 1024], BF16)
    xn32 = p.sbuf("xn32", [128, 1024], F32)
    xnf32_fm = p.sbuf("xnf32_fm", [128, 1024], F32)
    gates = p.sbuf("gates", [128, 32 * 8], F32)
    gbuf = [p.sbuf(f"gbuf{i}", [128, 512], BF16) for i in range(2)]
    g0buf = p.sbuf("g0buf", [128, 1024], F32)
    prodbuf2 = [p.sbuf(f"prodbuf{i}", [128, 1024], F32) for i in range(2)]
    stat = p.sbuf("stat", [128, 16], F32)
    probs_u = p.sbuf("probs_u", [128, 8], F32)
    probs = p.sbuf("probs", [128, 8], F32)
    mask1 = p.sbuf("mask1", [128, 8], F32)
    pmm = [p.psum(f"pmm{i}", [128, 512], F32) for i in range(6)]
    memset(p, "dve", stat, stat[:, 14:15], 1e-5)
    for dc in range(8):
        transpose(p, pmm[dc % 2], pmm[dc % 2].h[:, 0:8], router_sb,
                  router_sb.h[0:8, 128 * dc:128 * dc + 128], cst["ident_f32"],
                  kdim=8)
        copy(p, "dve", routerT, routerT[:, 8 * dc:8 * dc + 8],
             pmm[dc % 2], pmm[dc % 2].h[:, 0:8])
    wgT_d2 = [p.dram(f"wgT_d{e}", [16, 128, 8, 256], BF16) for e in range(8)]
    w1T_d2 = [p.dram(f"w1T_d{e}", [16, 128, 8, 256], BF16) for e in range(8)]
    w2T_d = [p.dram(f"w2T_d{e}", [4096, 1024], BF16) for e in range(8)]
    stg = [p.sbuf(f"stg{i}", [128, 4096], BF16) for i in range(3)]
    stg_k = [0]

    def stage_expert(e):
        dma(p, "pool", wg_bf, wg_bf[e, :, :], g["L0_ffn_wg_w"],
            g["L0_ffn_wg_w"][e, :, :], tag="precast", **MD)
        dma(p, "pool", w1_bf, w1_bf[e, :, :], g["L0_ffn_w1_w"],
            g["L0_ffn_w1_w"][e, :, :], tag="precast", **MD)
        dma(p, "pool", w2_bf, w2_bf[e, :, :], g["L0_ffn_w2_w"],
            g["L0_ffn_w2_w"][e, :, :], tag="precast", **MD)
        for (srcb, dstd) in ((wg_bf, wgT_d2[e]), (w1_bf, w1T_d2[e])):
            for dc in range(8):
                sb = stg[stg_k[0] % 3]; stg_k[0] += 1
                dma_tr(p, sb, sb.ap, srcb,
                       srcb[e, :, 128 * dc:128 * dc + 128])
                dma(p, "pool", dstd,
                    dstd[:, :, dc, :].rearrange("fs q f -> q fs f"),
                    sb, sb.ap.rearrange("q (fs f) -> q fs f", f=256),
                    tag="wtstore")
        for fq in range(8):
            sb = stg[stg_k[0] % 3]; stg_k[0] += 1
            for j in range(4):
                fc = 4 * fq + j
                dma_tr(p, sb, sb[:, 1024 * j:1024 * j + 1024],
                       w2_bf, w2_bf[e, :, 128 * fc:128 * fc + 128])
            dma(p, "pool", w2T_d[e],
                w2T_d[e][512 * fq:512 * fq + 512, :].rearrange(
                    "(c q) d -> q c d", q=128),
                sb, sb.ap.rearrange("q (c d) -> q c d", c=4), tag="wtstore")

    S_SUM, S_SQ, S_NM, S_RS, S_VAR, S_M8, S_NM8, S_S8, S_RI, S_M1, S_M2, \
        S_MX, S_NB, S_LI, S_EPS = range(15)

    def col(idx):
        return stat[:, idx:idx + 1]

    dbg_bufs = {}
    def dump(key, buf, ap, shape, dt=F32):
        if key in dbg:
            b = Buf(key, nc.dram_tensor("dbg_" + key, list(shape), dt,
                                        kind="ExternalOutput"))
            dbg_bufs[key] = b
            dma(p, "sp", b, b[:, :], buf, ap, tag="dbg")

    for w in range(NW):
        i0 = NT * w
        for il in range(NT):
            i = i0 + il
            dma(p, "sp", costl, costl[:, 64 * il:64 * il + 64], cst["_ccos_d"],
                cst["_ccos_d"][128 * i:128 * i + 128, :])
            dma(p, "sp", sintl, sintl[:, 64 * il:64 * il + 64], cst["_csin_d"],
                cst["_csin_d"][128 * i:128 * i + 128, :])
        for il in range(NT):
            i = i0 + il
            xt = xtile[il % 2]
            dma(p, "sp", xt, xt.ap, g["L0_attn_x"],
                g["L0_attn_x"][128 * i:128 * i + 128, :], tag="xload")
            act(p, junk, junk.ap, xt, xt.ap, AF.Square,
                accum=stat, accum_ap=col(S_SQ), tag="ln0sq")
            reduce(p, stat, col(S_SUM), xt, xt.ap, ALU.add, tag="ln0sum")
            ts(p, "dve", stat, col(S_NM), stat, col(S_SUM), -1.0 / D, ALU.mult)
            ts(p, "dve", stat, col(S_VAR), stat, col(S_NM), col(S_NM), ALU.mult)
            stt(p, "dve", stat, col(S_VAR), stat, col(S_SQ), 1.0 / D,
                stat, col(S_VAR), ALU.mult, ALU.subtract)
            act(p, stat, col(S_RS), stat, col(S_VAR), AF.Ln, bias=col(S_EPS))
            act(p, stat, col(S_RS), stat, col(S_RS), AF.Exp, scale=-0.5)
            ts(p, "dve", xn32, xn32.ap, xt, xt.ap, col(S_NM), ALU.add,
               col(S_RS), ALU.mult, extra_reads=(stat,), tag="ln0apply")
            xn_tm = junk
            copy(p, "dve", xn_tm, xn_tm.ap, xn32, xn32.ap)
            for dc in range(8):
                pt = ptr_bf[dc % 2]
                transpose(p, pt, pt.h[:, 0:128], xn_tm,
                          xn_tm[:, 128 * dc:128 * dc + 128], cst["ident"])
                copy(p, "dve", xn0_fm,
                     xn0_fm[:, dc * W + 128 * il:dc * W + 128 * il + 128],
                     pt, pt.h[:, 0:128])
            pk = pmm[il % 2]
            for dc in range(8):
                mm(p, pk, pk.h[:, 0:128],
                   xn0_fm, xn0_fm[:, dc * W + 128 * il:dc * W + 128 * il + 128],
                   WkT, WkT[:, 128 * dc:128 * dc + 128],
                   start=(dc == 0), stop=(dc == 7), tag="kproj")
            _rope(p, pk, krope[0], costl, sintl, il, ropebuf, ropebuf2)
            pt = ptr_bf[0]
            transpose(p, pt, pt.h[:, 0:128], krope[0], krope[0].ap,
                      cst["ident"])
            copy(p, "dve", K_fm, K_fm[:, 128 * i:128 * i + 128], pt,
                 pt.h[:, 0:128])
            pv = pmm[il % 2]
            for dc in range(8):
                mm(p, pv, pv.h[:, 0:128],
                   xn0_fm, xn0_fm[:, dc * W + 128 * il:dc * W + 128 * il + 128],
                   WvT, WvT[:, 128 * dc:128 * dc + 128],
                   start=(dc == 0), stop=(dc == 7), tag="vproj")
            copy(p, "dve", V_ext, V_ext[:, 132 * i:132 * i + 128], pv,
                 pv.h[:, 0:128])
            for dc in range(8):
                pz = pmm[dc % 2]
                transpose(p, pz, pz.h[:, 0:128], xn32,
                          xn32[:, 128 * dc:128 * dc + 128], cst["ident_f32"])
                copy(p, "dve", xnf32_fm,
                     xnf32_fm[:, 128 * dc:128 * dc + 128], pz, pz.h[:, 0:128])
            pr = pmm[il % 2]
            for dc in range(8):
                mm(p, pr, pr.h[:, 0:8],
                   xnf32_fm, xnf32_fm[:, 128 * dc:128 * dc + 128],
                   routerT, routerT[:, 8 * dc:8 * dc + 8],
                   start=(dc == 0), stop=(dc == 7), tag="router")
            reduce(p, stat, col(S_M8), pr, pr.h[:, 0:8], ALU.max)
            ts(p, "dve", stat, col(S_NM8), stat, col(S_M8), -1.0, ALU.mult)
            act(p, probs_u, probs_u.ap, pr, pr.h[:, 0:8], AF.Exp,
                bias=col(S_NM8), extra_reads=(stat,))
            reduce(p, stat, col(S_S8), probs_u, probs_u.ap, ALU.add)
            recip(p, stat, col(S_RI), stat, col(S_S8))
            ts(p, "dve", probs, probs.ap, probs_u, probs_u.ap, col(S_RI),
               ALU.mult, extra_reads=(stat,))
            reduce(p, stat, col(S_M1), probs, probs.ap, ALU.max)
            ts(p, "dve", mask1, mask1.ap, probs, probs.ap, col(S_M1),
               ALU.is_ge, extra_reads=(stat,))
            ts(p, "dve", mask1, mask1.ap, mask1, mask1.ap, -1.0, ALU.mult,
               1.0, ALU.add)
            tt(p, "dve", probs_u, probs_u.ap, probs, probs.ap, mask1,
               mask1.ap, ALU.mult)
            reduce(p, stat, col(S_M2), probs_u, probs_u.ap, ALU.max)
            ts(p, "dve", mask1, mask1.ap, probs, probs.ap, col(S_M2),
               ALU.is_ge, extra_reads=(stat,))
            tt(p, "dve", gates, gates[:, 8 * i:8 * i + 8], probs, probs.ap,
               mask1, mask1.ap, ALU.mult)
        if w == 0:
            dump("xn0", xn0_fm, xn0_fm.ap, (128, 8 * 512), BF16)
            dump("gates0", gates, gates[:, 0:32], (128, 32))
        def emit_slab(idx, s):
            e, fh = divmod(idx, 2)
            fs = fh * 8 + s
            dma(p, "sp", wgT_sl[s % 2],
                wgT_sl[s % 2].ap.rearrange("q (dc f) -> q dc f", dc=8),
                wgT_d2[e], wgT_d2[e][fs, :, :, :], tag="wgsl")
            dma(p, "act", w1T_sl[s % 2],
                w1T_sl[s % 2].ap.rearrange("q (dc f) -> q dc f", dc=8),
                w1T_d2[e], w1T_d2[e][fs, :, :, :], tag="w1sl")
        if w == 0:
            stage_expert(0)
        emit_slab(0, 0)
        emit_slab(0, 1)
        for idx in range(16):
            e, fh = divmod(idx, 2)
            if fh == 1 and w == 0 and e + 1 < 8:
                stage_expert(e + 1)
            if fh == 0:
                w2br = w2b_row[e % 2]
                dma(p, "pool", w2br, w2br.h[0:1, :], g["L0_ffn_w2_b"],
                    g["L0_ffn_w2_b"][e:e + 1, :])
            else:
                w2br = w2b_row[e % 2]
            for q4 in range(4):
                f0 = 512 * (4 * fh + q4)
                dma(p, "act", w2T_buf,
                    w2T_buf[:, 4096 * q4:4096 * q4 + 4096].rearrange(
                        "q (c d) -> q c d", c=4),
                    w2T_d[e], w2T_d[e][f0:f0 + 512, :].rearrange(
                        "(c q) d -> q c d", q=128), tag="w2sl")
            for s in range(8):
                wsl = wgT_sl[s % 2]
                usl = w1T_sl[s % 2]
                fs = fh * 8 + s
                for fl in range(2):
                    fc = fs * 2 + fl
                    fi = s * 2 + fl
                    pG = pmm[fl % 2]
                    pU = pmm[2 + fl % 2]
                    for dc in range(8):
                        mm(p, pG, pG.ap, wsl,
                           wsl[:, 256 * dc + 128 * fl:256 * dc + 128 * fl + 128],
                           xn0_fm, xn0_fm[:, dc * W:dc * W + W],
                           start=(dc == 0), stop=(dc == 7), tag="moeG")
                    for dc in range(8):
                        mm(p, pU, pU.ap, usl,
                           usl[:, 256 * dc + 128 * fl:256 * dc + 128 * fl + 128],
                           xn0_fm, xn0_fm[:, dc * W:dc * W + W],
                           start=(dc == 0), stop=(dc == 7), tag="moeU")
                    gb = gbuf[fl % 2]
                    act(p, gb, gb.ap, pG, pG.ap, AF.Gelu,
                        bias=wgb_cols[:, 32 * e + fc:32 * e + fc + 1],
                        extra_reads=(wgb_cols,), tag="gelu")
                    stt(p, "dve", h_buf, h_buf[:, fi * W:fi * W + W],
                        pU, pU.ap, w1b_cols[:, 32 * e + fc:32 * e + fc + 1],
                        gb, gb.ap, ALU.add, ALU.mult,
                        extra_reads=(w1b_cols,), tag="hmoe")
                if s + 2 < 8:
                    emit_slab(idx, s + 2)
                elif idx + 1 < 16:
                    emit_slab(idx + 1, s + 2 - 8)
            for tsub in range(NT):
                i = i0 + tsub
                gcol = gates[:, 8 * i + e:8 * i + e + 1]
                for half in range(2):
                    po = pmm[4 + half]
                    first = True
                    if fh == 0:
                        mm(p, po, po.ap, cst["ones1"], cst["ones1"].ap,
                           w2br, w2br.h[0:1, 512 * half:512 * half + 512],
                           start=True, stop=False, tag="w2bias")
                        first = False
                    for fi in range(16):
                        mm(p, po, po.ap, h_buf,
                           h_buf[:, fi * W + 128 * tsub:fi * W + 128 * tsub + 128],
                           w2T_buf,
                           w2T_buf[:, 1024 * fi + 512 * half:
                                   1024 * fi + 512 * half + 512],
                           start=(first and fi == 0),
                           stop=(fi == 15), tag="moeO")
                    aap = acc[:, 1024 * tsub + 512 * half:
                              1024 * tsub + 512 * half + 512]
                    if e == 0 and fh == 0:
                        ts(p, "dve", acc, aap, po, po.ap, gcol, ALU.mult,
                           extra_reads=(gates,), tag="accw")
                    else:
                        stt(p, "dve", acc, aap, po, po.ap, gcol,
                            acc, aap, ALU.mult, ALU.add,
                            extra_reads=(gates,), tag="acca")
        for il in range(NT):
            i = i0 + il
            slen = 128 * (i + 1)
            for h in range(2):
                pq = pmm[0]
                for dc in range(8):
                    mm(p, pq, pq.h[:, 0:128],
                       xn0_fm, xn0_fm[:, dc * W + 128 * il:dc * W + 128 * il + 128],
                       WqT, WqT[:, (h * 8 + dc) * 128:(h * 8 + dc) * 128 + 128],
                       start=(dc == 0), stop=(dc == 7), tag="qproj")
                _rope(p, pq, krope[1], costl, sintl, il, ropebuf, ropebuf2)
                pt = ptr_bf[1]
                transpose(p, pt, pt.h[:, 0:128], krope[1], krope[1].ap,
                          cst["ident"])
                copy(p, "dve", qfm, qfm.ap, pt, pt.h[:, 0:128])
                nch = (slen + 511) // 512
                for sc in range(nch):
                    wdt = min(512, slen - 512 * sc)
                    ps = pmm[sc % 2]
                    mm(p, ps, ps.h[:, 0:wdt], qfm, qfm.ap,
                       K_fm, K_fm[:, 512 * sc:512 * sc + wdt],
                       start=True, stop=True, tag="scores")
                    last = (sc == nch - 1)
                    if last:
                        if wdt > 128:
                            copy(p, "dve", scores,
                                 scores[:, 512 * sc:512 * sc + wdt - 128],
                                 ps, ps.h[:, 0:wdt - 128])
                        tt(p, "dve", scores, scores[:, slen - 128:slen],
                           ps, ps.h[:, wdt - 128:wdt], cst["mask"],
                           cst["mask"].ap, ALU.add)
                    else:
                        copy(p, "dve", scores,
                             scores[:, 512 * sc:512 * sc + wdt],
                             ps, ps.h[:, 0:wdt])
                reduce(p, stat, col(S_MX), scores, scores[:, 0:slen], ALU.max)
                ts(p, "dve", stat, col(S_NB), stat, col(S_MX), -SCALE0,
                   ALU.mult)
                pa = pmm[4]
                for sc in range(nch):
                    wdt = min(512, slen - 512 * sc)
                    pb = Pb[sc % 2]
                    act(p, pb, pb.h[:, 0:wdt], scores,
                        scores[:, 512 * sc:512 * sc + wdt], AF.Exp,
                        bias=col(S_NB), scale=SCALE0, extra_reads=(stat,),
                        tag="pexp")
                    for jj in range(wdt // 128):
                        j = sc * 4 + jj
                        pt2 = ptr_bf[j % 2]
                        transpose(p, pt2, pt2.h[:, 0:128], pb,
                                  pb.h[:, 128 * jj:128 * jj + 128],
                                  cst["ident"])
                        ptsb = PTb[j % 2]
                        copy(p, "dve", ptsb, ptsb.ap, pt2, pt2.h[:, 0:128])
                        mm(p, pa, pa.h[:, 0:129], ptsb, ptsb.ap,
                           V_ext, V_ext[:, 132 * j:132 * j + 129],
                           start=(j == 0), stop=(j == i), tag="pv")
                recip(p, stat, col(S_LI), pa, pa.h[:, 128:129])
                ts(p, "dve", atm, atm.ap, pa, pa.h[:, 0:128], col(S_LI),
                   ALU.mult, extra_reads=(stat,))
                pt3 = ptr_bf[h % 2]
                transpose(p, pt3, pt3.h[:, 0:128], atm, atm.ap, cst["ident"])
                copy(p, "dve", attn_fm, attn_fm[:, 128 * h:128 * h + 128],
                     pt3, pt3.h[:, 0:128])
            po_s = [pmm[2 + il % 2], pmm[4 + il % 2]]
            for half in range(2):
                for h in range(2):
                    mm(p, po_s[half], po_s[half].ap,
                       attn_fm, attn_fm[:, 128 * h:128 * h + 128],
                       WoT, WoT[:, 1024 * h + 512 * half:1024 * h + 512 * half + 512],
                       start=(h == 0), stop=(h == 1), tag="wo")
            act(p, g0buf, g0buf.ap, acc, acc[:, 1024 * il:1024 * il + 1024],
                AF.Silu, tag="silu0")
            xt2 = xtile[il % 2]
            dma(p, "sp", xt2, xt2.ap, g["L0_attn_x"],
                g["L0_attn_x"][128 * i:128 * i + 128, :], tag="xload2")
            pb2 = prodbuf2[il % 2]
            for half in range(2):
                tt(p, "dve", pb2, pb2.h[:, 512 * half:512 * half + 512],
                   po_s[half], po_s[half].ap, g0buf,
                   g0buf.h[:, 512 * half:512 * half + 512], ALU.mult)
            tt(p, "pool", pb2, pb2.ap, pb2, pb2.ap, xt2, xt2.ap, ALU.add)
            dma(p, "sp", x1_out, x1_out[128 * i:128 * i + 128, :], pb2,
                pb2.ap, tag="x1store")
    dump("Kfm", K_fm, K_fm.ap, (128, T), BF16)
    dump("Vx", V_ext, V_ext.ap, (128, 32 * 132), BF16)
    return dbg_bufs


def _rope(p, psrc, dst_bf, costl, sintl, il, rb, rb2):
    ce = costl[:, 64 * il:64 * il + 64]
    se = sintl[:, 64 * il:64 * il + 64]
    ev = psrc.h[:, 0:128:2]
    od = psrc.h[:, 1:128:2]
    tt(p, "dve", rb, rb.h[:, 0:64], psrc, ev, costl, ce, ALU.mult)
    tt(p, "dve", rb, rb.h[:, 64:128], psrc, od, sintl, se, ALU.mult)
    tt(p, "dve", dst_bf, dst_bf.h[:, 0:128:2], rb, rb.h[:, 0:64], rb,
       rb.h[:, 64:128], ALU.subtract)
    tt(p, "dve", rb2, rb2.h[:, 0:64], psrc, od, costl, ce, ALU.mult)
    tt(p, "dve", rb2, rb2.h[:, 64:128], psrc, ev, sintl, se, ALU.mult)
    tt(p, "dve", dst_bf, dst_bf.h[:, 1:128:2], rb2, rb2.h[:, 0:64], rb2,
       rb2.h[:, 64:128], ALU.add)



def make_l0_program(dbg=(), nw_limit=None):
    nc = bass.Bass("TRN2", target_bir_lowering=False, debug=False)
    p = Prog(nc)
    cst = declare_constants(p)
    g = declare_l0_inputs(nc)
    x1 = Buf("x1", nc.dram_tensor("x1", [T, D], F32, kind="ExternalOutput"))
    build_l0(p, g, cst, x1, dbg=dbg, nw_limit=nw_limit)
    p.emit(verbose=True)
    return nc


def l0_input_map(inputs, core):
    c = host_constants()
    m = dict(c)
    m["L0_attn_x"] = np.ascontiguousarray(inputs["L0_attn_x"][core])
    for k in ("L0_attn_Wq", "L0_attn_Wk", "L0_attn_Wv", "L0_attn_Wo",
              "L0_ffn_router_w", "L0_ffn_w1_w", "L0_ffn_w1_b", "L0_ffn_w2_w",
              "L0_ffn_w2_b", "L0_ffn_wg_w", "L0_ffn_wg_b"):
        v = np.ascontiguousarray(inputs[k])
        if k in ("L0_ffn_w1_b", "L0_ffn_wg_b"):
            v = v.reshape(8, 32, 128)
        m[k] = v
    return m



def declare_l1_inputs(nc):
    g = {}
    def di(name, shape):
        g[name] = Buf(name, nc.dram_tensor(name, list(shape), F32,
                                           kind="ExternalInput"))
    di("L1_attn_conv_w", (D, 7))
    di("L1_attn_norm1_w", (8, 128))
    di("L1_attn_norm2_w", (8, 128))
    di("L1_attn_Wdecay", (D, D))
    di("L1_attn_bdecay", (8, 128))
    di("L1_attn_Wgate", (D, D))
    di("L1_attn_bgate", (8, 128))
    di("L1_attn_state_scale", (1, 1))
    di("L1_attn_Wk", (D, 256))
    di("L1_attn_M", (256, 256))
    di("L1_attn_Wmem", (256, D))
    di("L1_attn_mem_scale", (1, 1))
    di("L1_attn_Wc", (3 * D, D))
    di("L1_attn_bc", (1, D))
    di("L1_attn_Wup", (D, 4096))
    di("L1_attn_Wg", (D, 4096))
    di("L1_attn_Wdown", (4096, D))
    di("L1_ffn_weight", (D, D))
    di("L1_ffn_aux_weight1", (D, D))
    di("L1_ffn_aux_weight2", (D, D))
    di("L1_ffn_bias", (1, D))
    return g


def _vec_cols_f32(p, name, src_buf, src_ap_2d, nchunk, cst, pf32):
    nat = p.sbuf(name + "_nat", [max(nchunk, 1), 128], F32)
    out = p.sbuf(name, [128, nchunk], F32)
    dma(p, "sp", nat, nat.h[0:nchunk, :], src_buf, src_ap_2d)
    idf = cst["ident_f32"]
    transpose(p, pf32, pf32.h[:, 0:nchunk], nat, nat.h[0:nchunk, :],
              idf, kdim=nchunk)
    copy(p, "dve", out, out[:, 0:nchunk], pf32, pf32.h[:, 0:nchunk])
    return out


def build_l1(p, g, cst, x1_in, out_d, dbg=()):
    nc = p.nc
    dbg_bufs = {}
    def dump(key, buf, ap, shape, dt=F32):
        if key in dbg:
            b = Buf(key, nc.dram_tensor("dbg_" + key, list(shape), dt,
                                        kind="ExternalOutput"))
            dbg_bufs[key] = b
            dma(p, "sp", b, b[:, :], buf, ap, tag="dbg")
    TW = 512
    NTW = T // TW
    NT = 4

    pw = [p.psum(f"l1pw{i}", [128, 1024], F32) for i in range(2)]
    pmms = [p.psum(f"l1pmm{i}", [128, 512], F32) for i in range(2)]
    pmm = pmms[0]
    ptrs = [p.psum(f"l1ptr{i}", [128, 1024], BF16) for i in range(2)]
    ptr = ptrs[0]

    MD = dict(max_dma_last_dim=8192)
    def precast(nm, shape):
        b = p.dram(nm + "_bf", list(shape), BF16)
        dma(p, "pool", b, b[:, :], g[nm], g[nm][:, :], tag="precast", **MD)
        return b
    wdec_bf = precast("L1_attn_Wdecay", (D, D))
    wgat_bf = precast("L1_attn_Wgate", (D, D))
    wk1_bf = precast("L1_attn_Wk", (D, 256))
    m_bf = precast("L1_attn_M", (256, 256))
    wmem_bf = precast("L1_attn_Wmem", (256, D))
    wc_bf = precast("L1_attn_Wc", (3 * D, D))
    wup_bf = precast("L1_attn_Wup", (D, 4096))
    wgg_bf = precast("L1_attn_Wg", (D, 4096))
    wdn_bf = precast("L1_attn_Wdown", (4096, D))
    ffw_bf = precast("L1_ffn_weight", (D, D))
    fa1_bf = precast("L1_ffn_aux_weight1", (D, D))
    fa2_bf = precast("L1_ffn_aux_weight2", (D, D))

    wk_sb = p.sbuf("wk_sb", [128, 8 * 256], BF16)
    for dc in range(8):
        dma(p, "sp", wk_sb, wk_sb[:, 256 * dc:256 * dc + 256],
            wk1_bf, wk1_bf[128 * dc:128 * dc + 128, :])
    MT_sb = p.sbuf("MT_sb", [128, 2 * 256], BF16)
    for ic in range(2):
        dma_tr(p, MT_sb, MT_sb[:, 256 * ic:256 * ic + 256],
               m_bf, m_bf[0:256, 128 * ic:128 * ic + 128])
    wmem_sb = p.sbuf("wmem_sb", [128, 2 * 1024], BF16)
    for jc in range(2):
        dma(p, "sp", wmem_sb, wmem_sb[:, 1024 * jc:1024 * jc + 1024],
            wmem_bf, wmem_bf[128 * jc:128 * jc + 128, :])
    idf = p.sbuf("ident_f32", [128, 128], F32)
    copy(p, "dve", idf, idf.ap, cst["ident"], cst["ident"].ap)
    cst["ident_f32"] = idf
    n1w_cols = _vec_cols_f32(p, "n1w", g["L1_attn_norm1_w"],
                             g["L1_attn_norm1_w"][:, :], 8, cst, pmms[1])
    n2w_cols = _vec_cols_f32(p, "n2w", g["L1_attn_norm2_w"],
                             g["L1_attn_norm2_w"][:, :], 8, cst, pmms[1])
    bdec_cols = _vec_cols_f32(p, "bdec", g["L1_attn_bdecay"],
                              g["L1_attn_bdecay"][:, :], 8, cst, pmms[1])
    bgat_cols = _vec_cols_f32(p, "bgat", g["L1_attn_bgate"],
                              g["L1_attn_bgate"][:, :], 8, cst, pmms[1])
    conv_sb = p.sbuf("conv_sb", [128, 8 * 7], F32)
    for dc in range(8):
        dma(p, "sp", conv_sb, conv_sb[:, 7 * dc:7 * dc + 7],
            g["L1_attn_conv_w"], g["L1_attn_conv_w"][128 * dc:128 * dc + 128, :])
    bc_row = p.sbuf("bc_row", [1, 1024], BF16)
    dma(p, "pool", bc_row, bc_row.h[0:1, :], g["L1_attn_bc"],
        g["L1_attn_bc"][:, :])
    ffb_row = p.sbuf("ffb_row", [1, 1024], BF16)
    dma(p, "pool", ffb_row, ffb_row.h[0:1, :], g["L1_ffn_bias"],
        g["L1_ffn_bias"][:, :])
    ones_f32 = p.sbuf("ones_f32", [1, 128], F32)
    copy(p, "dve", ones_f32, ones_f32.h[0:1, :], cst["ones1"],
         cst["ones1"].h[0:1, :])
    sc_in = p.sbuf("sc_in", [1, 2], F32)
    dma(p, "sp", sc_in, sc_in.h[0:1, 0:1], g["L1_attn_state_scale"],
        g["L1_attn_state_scale"][:, :])
    dma(p, "sp", sc_in, sc_in.h[0:1, 1:2], g["L1_attn_mem_scale"],
        g["L1_attn_mem_scale"][:, :])
    sc_cols = p.sbuf("sc_cols", [128, 2], F32)
    mm(p, pmm, pmm.h[:, 0:2], ones_f32, ones_f32.h[0:1, :], sc_in,
       sc_in.h[0:1, 0:2], start=True, stop=True, tag="scbcast")
    copy(p, "dve", sc_cols, sc_cols.ap, pmm, pmm.h[:, 0:2])
    ss_col = sc_cols[:, 0:1]
    ms_col = sc_cols[:, 1:2]

    slab = [p.sbuf(f"l1slab{i}", [128, 8 * 1024], BF16) for i in range(2)]
    rch = [p.sbuf(f"l1rch{i}", [128, 4 * 1024], BF16) for i in range(2)]
    gsl = [p.sbuf(f"l1gsl{i}", [128, 8 * 512], BF16) for i in range(2)]
    usl = [p.sbuf(f"l1usl{i}", [128, 8 * 512], BF16) for i in range(2)]

    x1t = [p.sbuf(f"x1t{i}", [128, 1024], F32) for i in range(2)]
    xn1_tm = p.sbuf("xn1_tm", [128, 4 * 1024], F32)
    xn1_fm = p.sbuf("xn1_fm", [128, 8 * TW], BF16)
    h_fm = p.sbuf("h_fm", [128, 8 * 518], BF16)
    cscr = p.sbuf("cscr", [128, 7 * 512], BF16)
    local_fm = p.sbuf("local_fm", [128, 8 * TW], BF16)
    state_fm = p.sbuf("state_fm", [128, 8 * TW], BF16)
    mem_fm = p.sbuf("mem_fm", [128, 8 * TW], BF16)
    k_fm = p.sbuf("k_fm", [128, 2 * TW], BF16)
    memo_fm = p.sbuf("memo_fm", [128, 2 * TW], BF16)
    dbuf = p.sbuf("dbuf", [128, TW], F32)
    gbuf2 = p.sbuf("gbuf2", [128, TW], F32)
    updb = p.sbuf("updb", [128, TW], F32)
    wrec = p.sbuf("wrec", [128, TW], F32)
    cumd = p.sbuf("cumd", [128, TW + 1], F32)
    cum2 = p.sbuf("cum2", [128, TW], F32)
    carry_cd = p.sbuf("carry_cd", [128, 8], F32)
    carry_c2 = p.sbuf("carry_c2", [128, 8], F32)
    h2_fm = state_fm
    ffin = mem_fm
    b_fm = local_fm
    silg = [p.sbuf(f"silg{i}", [128, TW], BF16) for i in range(2)]
    tb = [p.sbuf(f"tb{i}", [128, 1024], F32) for i in range(2)]
    gb = [p.sbuf(f"gb{i}", [128, 1024], F32) for i in range(2)]
    prodb = tb
    outb = gb
    junk1 = p.sbuf("junk1", [128, 1024], BF16)
    stat1 = p.sbuf("stat1", [128, 8], F32)
    L_SQ, L_RS, L_EPS1, L_EPS2 = range(4)

    def col(i):
        return stat1[:, i:i + 1]

    memset(p, "dve", stat1, col(L_EPS1), 1e-5)
    memset(p, "dve", stat1, col(L_EPS2), 1e-6)
    memset(p, "dve", carry_cd, carry_cd.ap, 1.0)
    memset(p, "dve", carry_c2, carry_c2.ap, 0.0)
    for dc in range(8):
        memset(p, "dve", h_fm, h_fm[:, 518 * dc:518 * dc + 6], 0.0)

    for it in range(NTW):
        t0 = TW * it
        for ts_ in range(NT):
            i = 4 * it + ts_
            xt = x1t[ts_ % 2]
            dma(p, "sp", xt, xt.ap, x1_in, x1_in[128 * i:128 * i + 128, :],
                tag="x1load")
            act(p, junk1, junk1.ap, xt, xt.ap, AF.Square,
                accum=stat1, accum_ap=col(L_SQ), tag="ln1sq")
            reduce(p, stat1, col(L_RS), xt, xt.ap, ALU.add)
            ts(p, "dve", stat1, col(L_RS), stat1, col(L_RS), 1.0 / D, ALU.mult)
            ts(p, "dve", stat1, col(L_SQ), stat1, col(L_SQ), 1.0 / D, ALU.mult)
            stt(p, "dve", stat1, col(L_SQ), stat1, col(L_RS), col(L_RS),
                stat1, col(L_SQ), ALU.mult, ALU.subtract)
            ts(p, "dve", stat1, col(L_SQ), stat1, col(L_SQ), -1.0, ALU.mult)
            act(p, stat1, col(L_SQ), stat1, col(L_SQ), AF.Ln, bias=col(L_EPS1))
            act(p, stat1, col(L_SQ), stat1, col(L_SQ), AF.Exp, scale=-0.5)
            ts(p, "dve", stat1, col(L_RS), stat1, col(L_RS), -1.0, ALU.mult)
            ts(p, "dve", xn1_tm, xn1_tm[:, 1024 * ts_:1024 * ts_ + 1024],
               xt, xt.ap, col(L_RS), ALU.add, col(L_SQ), ALU.mult,
               extra_reads=(stat1,), tag="ln1apply")
            copy(p, "dve", junk1, junk1.ap, xn1_tm,
                 xn1_tm[:, 1024 * ts_:1024 * ts_ + 1024])
            for dc in range(8):
                ptx = ptrs[dc % 2]
                transpose(p, ptx, ptx.h[:, 0:128], junk1,
                          junk1[:, 128 * dc:128 * dc + 128], cst["ident"])
                copy(p, "dve", xn1_fm,
                     xn1_fm[:, TW * dc + 128 * ts_:TW * dc + 128 * ts_ + 128],
                     ptx, ptx.h[:, 0:128])
        for ts_ in range(NT):
            i = 4 * it + ts_
            xns = xn1_tm[:, 1024 * ts_:1024 * ts_ + 1024]
            act(p, junk1, junk1.ap, xn1_tm, xns, AF.Square,
                accum=stat1, accum_ap=col(L_SQ), tag="rms1sq")
            ts(p, "dve", stat1, col(L_SQ), stat1, col(L_SQ), 1.0 / D, ALU.mult)
            act(p, stat1, col(L_SQ), stat1, col(L_SQ), AF.Ln, bias=col(L_EPS2))
            act(p, stat1, col(L_SQ), stat1, col(L_SQ), AF.Exp, scale=-0.5)
            ts(p, "dve", junk1, junk1.ap, xn1_tm, xns, col(L_SQ), ALU.mult,
               extra_reads=(stat1,), tag="rms1apply")
            if it == 0 and ts_ == 0:
                dump("rms1o", junk1, junk1.ap, (128, 1024), BF16)
                dump("n1wc", n1w_cols, n1w_cols.ap, (128, 8))
            for dc in range(8):
                ptx = ptrs[dc % 2]
                transpose(p, ptx, ptx.h[:, 0:128], junk1,
                          junk1[:, 128 * dc:128 * dc + 128], cst["ident"])
                ts(p, "dve", h_fm,
                   h_fm[:, 518 * dc + 6 + 128 * ts_:518 * dc + 6 + 128 * ts_ + 128],
                   ptx, ptx.h[:, 0:128], n1w_cols[:, dc:dc + 1], ALU.mult,
                   extra_reads=(n1w_cols,), tag="hevac")
        if it == 0:
            dump("xn1fm", xn1_fm, xn1_fm.ap, (128, 4096), BF16)
            dump("h", h_fm, h_fm.ap, (128, 8 * 518), BF16)
        dma(p, "sp", slab[0], slab[0].ap.rearrange("q (dc o) -> q dc o", dc=8),
            wdec_bf, wdec_bf[:, :].rearrange("(dc q) o -> q dc o", q=128),
            tag="wdec")
        dma(p, "act", slab[1], slab[1].ap.rearrange("q (dc o) -> q dc o", dc=8),
            wgat_bf, wgat_bf[:, :].rearrange("(dc q) o -> q dc o", q=128),
            tag="wgat")
        for oc in range(8):
            for dc in range(8):
                mm(p, pmm, pmm.ap, slab[0],
                   slab[0][:, 1024 * dc + 128 * oc:1024 * dc + 128 * oc + 128],
                   h_fm, h_fm[:, 518 * dc + 6:518 * dc + 6 + TW],
                   start=(dc == 0), stop=(dc == 7), tag="decmm")
            act(p, dbuf, dbuf.ap, pmm, pmm.ap, AF.Sigmoid,
                bias=bdec_cols[:, oc:oc + 1], extra_reads=(bdec_cols,),
                tag="sigd")
            ts(p, "dve", dbuf, dbuf.ap, dbuf, dbuf.ap, 0.05, ALU.max,
               0.95, ALU.min, tag="clip")
            if it == 0 and oc == 0:
                dump("d0", dbuf, dbuf.ap, (128, 512))
            for dc in range(8):
                mm(p, pmms[1], pmms[1].ap, slab[1],
                   slab[1][:, 1024 * dc + 128 * oc:1024 * dc + 128 * oc + 128],
                   h_fm, h_fm[:, 518 * dc + 6:518 * dc + 6 + TW],
                   start=(dc == 0), stop=(dc == 7), tag="gatmm")
            act(p, gbuf2, gbuf2.ap, pmms[1], pmms[1].ap, AF.Sigmoid,
                bias=bgat_cols[:, oc:oc + 1], extra_reads=(bgat_cols,),
                tag="sigg")
            tt(p, "dve", updb, updb.ap, gbuf2, gbuf2.ap, h_fm,
               h_fm[:, 518 * oc + 6:518 * oc + 6 + TW], ALU.mult, tag="upd")
            copy(p, "dve", cumd, cumd[:, 0:1], carry_cd,
                 carry_cd[:, oc:oc + 1])
            scan(p, cumd, cumd[:, 1:TW + 1], dbuf, dbuf.ap, dbuf, dbuf.ap,
                 carry_cd[:, oc:oc + 1], ALU.mult, ALU.bypass,
                 extra_reads=(carry_cd,), tag="cumprod")
            copy(p, "dve", carry_cd, carry_cd[:, oc:oc + 1], cumd,
                 cumd[:, TW:TW + 1])
            ts(p, "dve", wrec, wrec.ap, cumd, cumd[:, 0:TW], 1e-7, ALU.max,
               tag="wmax")
            recip(p, wrec, wrec.ap, wrec, wrec.ap)
            tt(p, "dve", wrec, wrec.ap, wrec, wrec.ap, updb, updb.ap,
               ALU.mult, tag="wmul")
            scan(p, cum2, cum2.ap, wrec, wrec.ap, wrec, wrec.ap,
                 carry_c2[:, oc:oc + 1], ALU.add, ALU.bypass,
                 extra_reads=(carry_c2,), tag="cumsum")
            copy(p, "dve", carry_c2, carry_c2[:, oc:oc + 1], cum2,
                 cum2[:, TW - 1:TW])
            stt(p, "dve", state_fm, state_fm[:, TW * oc:TW * oc + TW],
                cum2, cum2.ap, ss_col, cumd, cumd[:, 1:TW + 1],
                ALU.mult, ALU.mult, extra_reads=(sc_cols,), tag="state")
        if it == 0:
            dump("state", state_fm, state_fm.ap, (128, 4096), BF16)
            dump("cd0", cumd, cumd.ap, (128, 513))
        for kc in range(2):
            for dc in range(8):
                mm(p, pmm, pmm.ap, wk_sb,
                   wk_sb[:, 256 * dc + 128 * kc:256 * dc + 128 * kc + 128],
                   h_fm, h_fm[:, 518 * dc + 6:518 * dc + 6 + TW],
                   start=(dc == 0), stop=(dc == 7), tag="kmm")
            copy(p, "dve", k_fm, k_fm[:, TW * kc:TW * kc + TW], pmm, pmm.ap)
        for jc in range(2):
            for ic in range(2):
                mm(p, pmm, pmm.ap, MT_sb,
                   MT_sb[:, 256 * ic + 128 * jc:256 * ic + 128 * jc + 128],
                   k_fm, k_fm[:, TW * ic:TW * ic + TW],
                   start=(ic == 0), stop=(ic == 1), tag="memomm")
            copy(p, "dve", memo_fm, memo_fm[:, TW * jc:TW * jc + TW],
                 pmm, pmm.ap)
        for oc in range(8):
            for jc in range(2):
                mm(p, pmm, pmm.ap, wmem_sb,
                   wmem_sb[:, 1024 * jc + 128 * oc:1024 * jc + 128 * oc + 128],
                   memo_fm, memo_fm[:, TW * jc:TW * jc + TW],
                   start=(jc == 0), stop=(jc == 1), tag="memmm")
            act(p, mem_fm, mem_fm[:, TW * oc:TW * oc + TW], pmm, pmm.ap,
                AF.Copy, scale=ms_col, extra_reads=(sc_cols,), tag="memsc")
        if it == 0:
            dump("mem", mem_fm, mem_fm.ap, (128, 4096), BF16)
            dump("kk", k_fm, k_fm.ap, (128, 1024), BF16)
        for dc in range(8):
            base = 518 * dc
            for j in range(7):
                ts(p, "dve", cscr, cscr[:, 512 * j:512 * j + 512],
                   h_fm, h_fm[:, base + j:base + j + TW],
                   conv_sb[:, 7 * dc + j:7 * dc + j + 1], ALU.mult,
                   extra_reads=(conv_sb,), tag="convp")
            tt(p, "pool", cscr, cscr[:, 512 * 0:512 * 0 + 512],
               cscr, cscr[:, 0:512], cscr, cscr[:, 512:1024], ALU.add,
               tag="conva")
            tt(p, "pool", cscr, cscr[:, 512 * 2:512 * 2 + 512],
               cscr, cscr[:, 1024:1536], cscr, cscr[:, 1536:2048], ALU.add,
               tag="conva")
            tt(p, "pool", cscr, cscr[:, 512 * 4:512 * 4 + 512],
               cscr, cscr[:, 2048:2560], cscr, cscr[:, 2560:3072], ALU.add,
               tag="conva")
            tt(p, "pool", cscr, cscr[:, 0:512],
               cscr, cscr[:, 0:512], cscr, cscr[:, 1024:1536], ALU.add,
               tag="conva")
            tt(p, "pool", cscr, cscr[:, 2048:2560],
               cscr, cscr[:, 2048:2560], cscr, cscr[:, 3072:3584], ALU.add,
               tag="conva")
            tt(p, "pool", local_fm, local_fm[:, TW * dc:TW * dc + TW],
               cscr, cscr[:, 0:512], cscr, cscr[:, 2048:2560], ALU.add,
               tag="conva")
        srcs = [local_fm, state_fm, mem_fm]
        for ph in range(2):
            for cc in range(24):
                rb4 = rch[(cc // 4) % 2]
                if cc % 4 == 0:
                    dma(p, "sp", rb4,
                        rb4.ap.rearrange("q (c o) -> q c o", c=4),
                        wc_bf, wc_bf[128 * cc:128 * cc + 512, :].rearrange(
                            "(c q) o -> q c o", q=128), tag="wc")
                rb = rb4
                for k2 in range(2):
                    ts_ = 2 * ph + k2
                    pwx = pw[k2]
                    if cc == 0:
                        mm(p, pwx, pwx.h[:, 0:512], cst["ones1"],
                           cst["ones1"].ap, bc_row, bc_row.h[0:1, 0:512],
                           start=True, stop=False, tag="bcb")
                        mm(p, pwx, pwx.h[:, 512:1024], cst["ones1"],
                           cst["ones1"].ap, bc_row, bc_row.h[0:1, 512:1024],
                           start=True, stop=False, tag="bcb")
                    src = srcs[cc // 8]
                    lap = src[:, TW * (cc % 8) + 128 * ts_:
                              TW * (cc % 8) + 128 * ts_ + 128]
                    for half in range(2):
                        mm(p, pwx, pwx.h[:, 512 * half:512 * half + 512],
                           src, lap, rb,
                           rb[:, 1024 * (cc % 4) + 512 * half:
                              1024 * (cc % 4) + 512 * half + 512],
                           start=False, stop=(cc == 23 and half == 1),
                           tag="wcmm")
            for k2 in range(2):
                ts_ = 2 * ph + k2
                xns = xn1_tm[:, 1024 * ts_:1024 * ts_ + 1024]
                tt(p, "dve", xn1_tm, xns, xn1_tm, xns, pw[k2], pw[k2].ap,
                   ALU.add, tag="mixed")
        if it == 0:
            dump("mixed", xn1_tm, xn1_tm.ap, (128, 4096))
        for ts_ in range(NT):
            xns = xn1_tm[:, 1024 * ts_:1024 * ts_ + 1024]
            act(p, junk1, junk1.ap, xn1_tm, xns, AF.Square,
                accum=stat1, accum_ap=col(L_SQ), tag="rms2sq")
            ts(p, "dve", stat1, col(L_SQ), stat1, col(L_SQ), 1.0 / D, ALU.mult)
            act(p, stat1, col(L_SQ), stat1, col(L_SQ), AF.Ln, bias=col(L_EPS2))
            act(p, stat1, col(L_SQ), stat1, col(L_SQ), AF.Exp, scale=-0.5)
            ts(p, "dve", junk1, junk1.ap, xn1_tm, xns, col(L_SQ), ALU.mult,
               extra_reads=(stat1,), tag="rms2apply")
            for dc in range(8):
                ptx = ptrs[dc % 2]
                transpose(p, ptx, ptx.h[:, 0:128], junk1,
                          junk1[:, 128 * dc:128 * dc + 128], cst["ident"])
                ts(p, "dve", h2_fm,
                   h2_fm[:, TW * dc + 128 * ts_:TW * dc + 128 * ts_ + 128],
                   ptx, ptx.h[:, 0:128], n2w_cols[:, dc:dc + 1], ALU.mult,
                   extra_reads=(n2w_cols,), tag="h2evac")
        for fq in range(4):
            for s2 in range(2):
                fs = 2 * fq + s2
                gs, us = gsl[s2], usl[s2]
                dma(p, "sp", gs, gs.ap.rearrange("q (dc f) -> q dc f", dc=8),
                    wgg_bf, wgg_bf[:, 512 * fs:512 * fs + 512].rearrange(
                        "(dc q) f -> q dc f", q=128), tag="wg")
                dma(p, "act", us, us.ap.rearrange("q (dc f) -> q dc f", dc=8),
                    wup_bf, wup_bf[:, 512 * fs:512 * fs + 512].rearrange(
                        "(dc q) f -> q dc f", q=128), tag="wu")
                for fl in range(4):
                    fi = 4 * s2 + fl
                    for dc in range(8):
                        mm(p, pmm, pmm.ap, gs,
                           gs[:, 512 * dc + 128 * fl:512 * dc + 128 * fl + 128],
                           h2_fm, h2_fm[:, TW * dc:TW * dc + TW],
                           start=(dc == 0), stop=(dc == 7), tag="gmm")
                    for dc in range(8):
                        mm(p, pmms[1], pmms[1].ap, us,
                           us[:, 512 * dc + 128 * fl:512 * dc + 128 * fl + 128],
                           h2_fm, h2_fm[:, TW * dc:TW * dc + TW],
                           start=(dc == 0), stop=(dc == 7), tag="umm")
                    sg = silg[fl % 2]
                    act(p, sg, sg.ap, pmm, pmm.ap, AF.Silu, tag="silu1")
                    tt(p, "dve", ffin, ffin[:, TW * fi:TW * fi + TW],
                       sg, sg.ap, pmms[1], pmms[1].ap, ALU.mult,
                       tag="ffinmul")
            for ph in range(2):
                for fi in range(8):
                    fc = 8 * fq + fi
                    rb4 = rch[(fi // 4) % 2]
                    if fi % 4 == 0:
                        dma(p, "sp", rb4,
                            rb4.ap.rearrange("q (c o) -> q c o", c=4),
                            wdn_bf,
                            wdn_bf[128 * fc:128 * fc + 512, :].rearrange(
                                "(c q) o -> q c o", q=128), tag="wdn")
                    for k2 in range(2):
                        ts_ = 2 * ph + k2
                        pwx = pw[k2]
                        for half in range(2):
                            mm(p, pwx, pwx.h[:, 512 * half:512 * half + 512],
                               ffin, ffin[:, TW * fi + 128 * ts_:
                                          TW * fi + 128 * ts_ + 128],
                               rb4, rb4[:, 1024 * (fi % 4) + 512 * half:
                                        1024 * (fi % 4) + 512 * half + 512],
                               start=(fi == 0), stop=(fi == 7 and half == 1),
                               tag="ffmm")
                for k2 in range(2):
                    ts_ = 2 * ph + k2
                    xns = xn1_tm[:, 1024 * ts_:1024 * ts_ + 1024]
                    tt(p, "dve", xn1_tm, xns, xn1_tm, xns, pw[k2],
                       pw[k2].ap, ALU.add, tag="ffadd")
        if it == 0:
            dump("h2", h2_fm, h2_fm.ap, (128, 4096), BF16)
            dump("smix", xn1_tm, xn1_tm.ap, (128, 4096))
        if it < NTW - 1:
            for dc in range(8):
                copy(p, "pool", h_fm, h_fm[:, 518 * dc:518 * dc + 6],
                     h_fm, h_fm[:, 518 * dc + TW:518 * dc + TW + 6],
                     tag="halo")
        for step in range(2):
            wsrc = ffw_bf if step == 0 else fa1_bf
            sl = slab[step]
            dma(p, "sp" if step == 0 else "act", sl,
                sl.ap.rearrange("q (dc o) -> q dc o", dc=8),
                wsrc, wsrc[:, :].rearrange("(dc q) o -> q dc o", q=128),
                tag="ffnw")
            for oc in range(8):
                for dc in range(8):
                    rap = (xn1_fm[:, TW * dc:TW * dc + TW] if step == 0 else
                           h_fm[:, 518 * dc + 6:518 * dc + 6 + TW])
                    rbuf = xn1_fm if step == 0 else h_fm
                    mm(p, pmm, pmm.ap, sl,
                       sl[:, 1024 * dc + 128 * oc:1024 * dc + 128 * oc + 128],
                       rbuf, rap, start=(dc == 0), stop=(dc == 7), tag="abmm")
                if step == 0:
                    copy(p, "dve", h_fm,
                         h_fm[:, 518 * oc + 6:518 * oc + 6 + TW], pmm, pmm.ap)
                else:
                    copy(p, "dve", local_fm,
                         local_fm[:, TW * oc:TW * oc + TW], pmm, pmm.ap)
        for ph in range(2):
            for cc in range(8):
                rb4 = rch[(cc // 4) % 2]
                if cc % 4 == 0:
                    dma(p, "sp", rb4,
                        rb4.ap.rearrange("q (c o) -> q c o", c=4),
                        fa2_bf, fa2_bf[128 * cc:128 * cc + 512, :].rearrange(
                            "(c q) o -> q c o", q=128), tag="fa2")
                rb = rb4
                for k2 in range(2):
                    ts_ = 2 * ph + k2
                    pwx = pw[k2]
                    if cc == 0:
                        for half in range(2):
                            mm(p, pwx, pwx.h[:, 512 * half:512 * half + 512],
                               cst["ones1"], cst["ones1"].ap, ffb_row,
                               ffb_row.h[0:1, 512 * half:512 * half + 512],
                               start=True, stop=False, tag="ffbb")
                    for half in range(2):
                        mm(p, pwx, pwx.h[:, 512 * half:512 * half + 512],
                           b_fm, b_fm[:, TW * cc + 128 * ts_:
                                      TW * cc + 128 * ts_ + 128],
                           rb, rb[:, 1024 * (cc % 4) + 512 * half:
                                  1024 * (cc % 4) + 512 * half + 512],
                           start=False, stop=(cc == 7 and half == 1),
                           tag="cmm")
            for k2 in range(2):
                ts_ = 2 * ph + k2
                i = 4 * it + ts_
                tbx, gbx, pbx, obx = tb[k2], gb[k2], tb[k2], gb[k2]
                act(p, tbx, tbx.ap, pw[k2], pw[k2].ap, AF.Tanh, tag="tanh")
                act(p, gbx, gbx.ap, tbx, tbx.ap, AF.Silu, tag="silu2")
                tt(p, "dve", pbx, pbx.ap, xn1_tm,
                   xn1_tm[:, 1024 * ts_:1024 * ts_ + 1024], gbx, gbx.ap,
                   ALU.mult, tag="sg")
                xt = x1t[k2]
                dma(p, "sp", xt, xt.ap, x1_in,
                    x1_in[128 * i:128 * i + 128, :], tag="x1load2")
                tt(p, "pool", obx, obx.ap, pbx, pbx.ap, xt, xt.ap, ALU.add,
                   tag="final")
                dma(p, "sp", out_d, out_d[128 * i:128 * i + 128, :], obx,
                    obx.ap, tag="outstore")


def make_l1_program(dbg=(), paranoid=False):
    nc = bass.Bass("TRN2", target_bir_lowering=False, debug=False)
    p = Prog(nc)
    cst = declare_constants(p)
    g = declare_l1_inputs(nc)
    x1 = Buf("x1", nc.dram_tensor("x1", [T, D], F32, kind="ExternalInput"))
    out = Buf("out", nc.dram_tensor("out", [T, D], F32, kind="ExternalOutput"))
    build_l1(p, g, cst, x1, out, dbg=dbg)
    p.emit(verbose=True, paranoid=paranoid)
    return nc


L1_RESHAPE = {"L1_attn_norm1_w": (8, 128), "L1_attn_norm2_w": (8, 128),
              "L1_attn_bdecay": (8, 128), "L1_attn_bgate": (8, 128),
              "L1_attn_state_scale": (1, 1), "L1_attn_mem_scale": (1, 1),
              "L1_attn_bc": (1, D), "L1_ffn_bias": (1, D)}


def l1_input_map(inputs, core, x1):
    c = host_constants()
    m = dict(c)
    m["x1"] = np.ascontiguousarray(x1)
    for k in inputs:
        if k.startswith("L1_"):
            v = np.ascontiguousarray(inputs[k])
            if k in L1_RESHAPE:
                v = v.reshape(L1_RESHAPE[k])
            m[k] = v
    return m



def make_fused_program(dbg=()):
    nc = bass.Bass("TRN2", target_bir_lowering=False, debug=False)
    p = Prog(nc)
    cst = declare_constants(p)
    g0 = declare_l0_inputs(nc)
    g1 = declare_l1_inputs(nc)
    x1 = p.dram("x1s", [T, D], F32, kind="Internal")
    out = Buf("out", nc.dram_tensor("out", [T, D], F32, kind="ExternalOutput"))
    p.push_scope()
    build_l0(p, g0, cst, x1, dbg=dbg)
    p.barrier()
    p.pop_scope()
    p.push_scope()
    build_l1(p, g1, cst, x1, out, dbg=dbg)
    p.pop_scope()
    p.emit(verbose=True)
    return nc


def fused_input_map(inputs, core):
    m = l0_input_map(inputs, core)
    for k in inputs:
        if k.startswith("L1_"):
            v = np.ascontiguousarray(inputs[k])
            if k in L1_RESHAPE:
                v = v.reshape(L1_RESHAPE[k])
            m[k] = v
    return m


def kernel(**inputs):
    inputs = {k: np.asarray(v) for k, v in inputs.items()}
    core_ids = list(range(NCORES))
    nc = make_fused_program()
    maps = [fused_input_map(inputs, c) for c in core_ids]
    res = run_bass_kernel_spmd(nc, maps, core_ids=core_ids)
    out = np.stack([res.results[c]["out"] for c in core_ids], axis=0)
    return out.astype(np.float32)


# version 11
# speedup vs baseline: 1.1326x; 1.1326x over previous
import numpy as np
import ml_dtypes
from contextlib import ExitStack

import concourse.bass as bass
import concourse.mybir as mybir
from concourse.bass_utils import run_bass_kernel_spmd

F32 = mybir.dt.float32
BF16 = mybir.dt.bfloat16
AF = mybir.ActivationFunctionType
ALU = mybir.AluOpType
BF = ml_dtypes.bfloat16

T, D = 4096, 1024
NCORES = 8


class Buf:
    __slots__ = ("name", "h", "last_writer", "readers")

    def __init__(self, name, h):
        self.name = name
        self.h = h
        self.last_writer = None
        self.readers = []

    def __getitem__(self, sl):
        return self.h[sl]

    @property
    def ap(self):
        return self.h[:]


class _Op:
    __slots__ = ("idx", "eng", "fn", "deps", "waw", "need_inc", "is_dma",
                 "inc_val", "engine_pos", "tag")


ENGINES = ("pe", "act", "dve", "pool", "sp")
RPOOL = {"sp": 20, "pool": 8, "act": 8, "dve": 4, "pe": 4}
BLOCK_ATTR = {"pe": "tensor", "act": "scalar", "dve": "vector",
              "pool": "gpsimd", "sp": "sync"}


class Prog:
    def __init__(self, nc):
        self.nc = nc
        self.ops = []
        self.stack = ExitStack()
        self._scopes = []
        self.sbuf_bytes = 0
        self._names = set()

    def push_scope(self):
        self._scopes.append(ExitStack())

    def pop_scope(self):
        self._scopes.pop().close()

    @property
    def _alloc_stack(self):
        return self._scopes[-1] if self._scopes else self.stack

    def _uname(self, name):
        i = 0
        n = name
        while n in self._names:
            i += 1
            n = f"{name}_{i}"
        self._names.add(n)
        return n

    def sbuf(self, name, shape, dtype):
        name = self._uname(name)
        h = self._alloc_stack.enter_context(
            self.nc.sbuf_tensor(name, shape, dtype))
        esz = 4 if dtype == F32 else 2
        self.sbuf_bytes += shape[1] * esz
        return Buf(name, h)

    def psum(self, name, shape, dtype):
        name = self._uname(name)
        h = self._alloc_stack.enter_context(
            self.nc.psum_tensor(name, shape, dtype))
        return Buf(name, h)

    def dram(self, name, shape, dtype, kind="Internal"):
        name = self._uname(name)
        h = self.nc.dram_tensor(name, shape, dtype, kind=kind)
        return Buf(name, h)

    def op(self, eng, fn, reads=(), writes=(), dma=False, extra_deps=(), tag=""):
        i = len(self.ops)
        o = _Op()
        o.idx, o.eng, o.fn, o.is_dma, o.tag = i, eng, fn, dma, tag
        o.need_inc = dma
        deps = set(extra_deps)
        waw = set()
        for b in reads:
            if b.last_writer is not None:
                deps.add(b.last_writer)
            b.readers.append(i)
        for b in writes:
            if b.last_writer is not None:
                waw.add(b.last_writer)
            deps.update(b.readers)
            b.last_writer = i
            b.readers = []
        waw -= deps
        deps.discard(i)
        waw.discard(i)
        o.deps = deps
        o.waw = waw
        self.ops.append(o)
        return i

    def barrier(self):
        last_c = {}
        last_d = {e: [] for e in ENGINES}
        for o in self.ops:
            if o.is_dma:
                last_d[o.eng].append(o.idx)
            else:
                last_c[o.eng] = o.idx
        deps = set(last_c.values())
        for e, lst in last_d.items():
            deps.update(lst[-RPOOL[e]:])
        used = sorted({o.eng for o in self.ops})

        def f(eng):
            eng.drain()
            return eng.nop()
        firsts = [self.op(e, f, extra_deps=tuple(deps), tag="barrier1")
                  for e in used]
        for e in used:
            self.op(e, lambda eng: eng.nop(), extra_deps=tuple(firsts),
                    tag="barrier2")

    def emit(self, verbose=False, paranoid=False):
        ops = self.ops
        if paranoid:
            for i, o in enumerate(ops):
                o.deps = set(range(i))
        for o in ops:
            for d in o.deps | o.waw:
                p = ops[d]
                if p.is_dma or p.eng != o.eng:
                    p.need_inc = True
        cnt_c = {e: 0 for e in ENGINES}
        cnt_d = {e: 0 for e in ENGINES}
        pos = {e: 0 for e in ENGINES}
        for o in ops:
            o.engine_pos = pos[o.eng]
            pos[o.eng] += 1
            if o.need_inc:
                if o.is_dma:
                    o.inc_val = cnt_d[o.eng]
                    cnt_d[o.eng] += 1
                else:
                    cnt_c[o.eng] += 1
                    o.inc_val = cnt_c[o.eng]
        used = sorted({o.eng for o in ops})
        sems = {}
        for e in used:
            if any(o.need_inc and not o.is_dma for o in ops if o.eng == e):
                sems[("c", e)] = self.stack.enter_context(
                    self.nc.semaphore(f"sem_c_{e}"))
            nd = cnt_d[e]
            if nd:
                for r in range(min(RPOOL[e], nd)):
                    sems[("d", e, r)] = self.stack.enter_context(
                        self.nc.semaphore(f"sem_d_{e}_{r}"))
        watermark = {e: {} for e in ENGINES}
        last_drain = {e: -1 for e in ENGINES}
        plans = []
        n_waits = 0
        n_drains = 0
        for o in ops:
            waits = []
            pre_drain = False
            gate = None
            if o.is_dma:
                R = RPOOL[o.eng]
                i = o.inc_val
                if i >= R:
                    gate = (("d", o.eng, i % R), 16 * (i // R))
            for d in sorted(o.deps | o.waw):
                p = ops[d]
                if (not p.is_dma) and p.eng == o.eng:
                    if o.eng in ("pe", "sp"):
                        continue
                    if d in o.waw:
                        continue
                    if p.engine_pos > last_drain[o.eng]:
                        pre_drain = True
                elif p.is_dma:
                    R = RPOOL[p.eng]
                    key = ("d", p.eng, p.inc_val % R)
                    v = 16 * (p.inc_val // R + 1)
                    if watermark[o.eng].get(key, 0) < v:
                        waits.append((key, v))
                else:
                    key = ("c", p.eng)
                    v = p.inc_val
                    if watermark[o.eng].get(key, 0) < v:
                        waits.append((key, v))
            if pre_drain:
                last_drain[o.eng] = o.engine_pos - 1
                n_drains += 1
            wd = {}
            for k, v in waits:
                wd[k] = max(wd.get(k, 0), v)
            if gate is not None and wd.get(gate[0], 0) < gate[1]:
                wd[gate[0]] = max(wd.get(gate[0], 0), gate[1])
            for k, v in wd.items():
                watermark[o.eng][k] = v
            plans.append((o, sorted(wd.items()), pre_drain))
            n_waits += len(wd)
        if verbose:
            per_eng = {e: sum(1 for o in ops if o.eng == e) for e in used}
            print(f"[prog] ops={len(ops)} per-engine={per_eng} waits={n_waits} "
                  f"drains={n_drains} sbuf_bytes/part={self.sbuf_bytes}")
        final_waits = []
        for e in used:
            nd = cnt_d[e]
            R = RPOOL[e]
            for r in range(min(R, nd)):
                uses = (nd - 1 - r) // R + 1
                final_waits.append((("d", e, r), 16 * uses))
            if ("c", e) in sems and cnt_c[e]:
                final_waits.append((("c", e), cnt_c[e]))

        eng_plans = {e: [] for e in used}
        for pl in plans:
            eng_plans[pl[0].eng].append(pl)

        blk = self.stack.enter_context(self.nc.Block())

        def mk_body(e, my_plans):
            def body(eng):
                for o, waits, pre_drain in my_plans:
                    for k, v in waits:
                        eng.wait_ge(sems[k], v)
                    if pre_drain:
                        eng.drain()
                    instr = o.fn(eng)
                    if o.need_inc:
                        if o.is_dma:
                            key = ("d", o.eng, o.inc_val % RPOOL[o.eng])
                            instr.then_inc(sems[key], 16)
                        else:
                            instr.then_inc(sems[("c", o.eng)], 1)
                return None
            return body

        for e in used:
            if e == "sp":
                continue
            getattr(blk, BLOCK_ATTR[e])(mk_body(e, eng_plans[e]))
        def sp_body(eng):
            for o, waits, pre_drain in eng_plans.get("sp", []):
                for k, v in waits:
                    eng.wait_ge(sems[k], v)
                instr = o.fn(eng)
                if o.need_inc:
                    if o.is_dma:
                        instr.then_inc(sems[("d", "sp", o.inc_val % RPOOL["sp"])], 16)
                    else:
                        instr.then_inc(sems[("c", "sp")], 1)
            for k, v in final_waits:
                eng.wait_ge(sems[k], v)
        blk.sync(sp_body)
        self.stack.close()



def dma(p, q, dst, dst_ap, src, src_ap, tag="", **kw):
    p.op(q, lambda e: e.dma_start(out=dst_ap, in_=src_ap, **kw),
         reads=(src,), writes=(dst,), dma=True, tag=tag)


def dma_tr(p, dst, dst_ap, src, src_ap, tag=""):
    p.op("sp", lambda e: e.dma_start_transpose(dst_ap, src_ap),
         reads=(src,), writes=(dst,), dma=True, tag=tag)


def mm(p, out, out_ap, lhsT, lhsT_ap, rhs, rhs_ap, start, stop, tag=""):
    p.op("pe", lambda e: e.matmul(out_ap, lhsT_ap, rhs_ap, start=start, stop=stop),
         reads=(lhsT, rhs), writes=(out,), tag=tag)


def transpose(p, out, out_ap, src, src_ap, ident, kdim=128, tag=""):
    id_ap = ident.ap if kdim == 128 else ident.h[0:kdim, 0:kdim]
    p.op("pe", lambda e: e.transpose(out_ap, src_ap, id_ap),
         reads=(src, ident), writes=(out,), tag=tag)


def act(p, out, out_ap, src, src_ap, func, bias=None, scale=None, accum=None,
        accum_ap=None, extra_reads=(), tag=""):
    kw = {}
    if bias is not None:
        kw["bias"] = bias
    if scale is not None:
        kw["scale"] = scale
    if accum is not None:
        kw["accum_out"] = accum_ap
    writes = [out] if out is not None else []
    if accum is not None:
        writes.append(accum)
    p.op("act", lambda e: e.activation(out=out_ap, in_=src_ap, func=func, **kw),
         reads=(src,) + tuple(extra_reads), writes=tuple(writes), tag=tag)
    if accum is not None:
        def _f(e):
            e.drain()
            return e.nop()
        p.op("act", _f, reads=(accum,), writes=(accum,), tag=tag + "_sync")


def tt(p, eng, out, out_ap, a, a_ap, b, b_ap, op, tag=""):
    p.op(eng, lambda e: e.tensor_tensor(out=out_ap, in0=a_ap, in1=b_ap, op=op),
         reads=(a, b), writes=(out,), tag=tag)


def ts(p, eng, out, out_ap, a, a_ap, s1, op0, s2=None, op1=None, extra_reads=(),
       tag=""):
    kw = dict(scalar1=s1, scalar2=s2, op0=op0)
    if op1 is not None:
        kw["op1"] = op1
    p.op(eng, lambda e: e.tensor_scalar(out=out_ap, in0=a_ap, **kw),
         reads=(a,) + tuple(extra_reads), writes=(out,), tag=tag)


def stt(p, eng, out, out_ap, a, a_ap, scalar, b, b_ap, op0, op1,
        extra_reads=(), tag=""):
    p.op(eng, lambda e: e.scalar_tensor_tensor(out=out_ap, in0=a_ap,
                                               scalar=scalar, in1=b_ap,
                                               op0=op0, op1=op1),
         reads=(a, b) + tuple(extra_reads), writes=(out,), tag=tag)


def copy(p, eng, out, out_ap, src, src_ap, tag=""):
    if eng == "act":
        p.op(eng, lambda e: e.copy(out=out_ap, in_=src_ap),
             reads=(src,), writes=(out,), tag=tag)
    else:
        p.op(eng, lambda e: e.tensor_copy(out=out_ap, in_=src_ap),
             reads=(src,), writes=(out,), tag=tag)


def reduce(p, out, out_ap, src, src_ap, op, negate=False, tag=""):
    p.op("dve", lambda e: e.tensor_reduce(out_ap, src_ap,
                                          mybir.AxisListType.X, op,
                                          negate=negate),
         reads=(src,), writes=(out,), tag=tag)


def scan(p, out, out_ap, d0, d0_ap, d1, d1_ap, initial, op0, op1,
         extra_reads=(), tag=""):
    p.op("dve", lambda e: e.tensor_tensor_scan(out=out_ap, data0=d0_ap,
                                               data1=d1_ap, initial=initial,
                                               op0=op0, op1=op1),
         reads=(d0, d1) + tuple(extra_reads), writes=(out,), tag=tag)


def memset(p, eng, buf, ap, val, tag=""):
    p.op(eng, lambda e: e.memset(ap, val), writes=(buf,), tag=tag)


def recip(p, out, out_ap, src, src_ap, tag=""):
    p.op("dve", lambda e: e.reciprocal(out_ap, src_ap),
         reads=(src,), writes=(out,), tag=tag)



def host_constants():
    c = {}
    c["c_ident"] = np.eye(128, dtype=BF)
    c["c_ones1"] = np.ones((1, 128), dtype=BF)
    m = np.zeros((128, 128), np.float32)
    m[np.triu_indices(128, 1)] = -1e9
    c["c_mask"] = m
    i = np.arange(64, dtype=np.float64)
    theta = 10000.0 ** (-i / 32.0)
    pos = np.arange(T, dtype=np.float64)
    ang = (pos[:, None].astype(np.float32).astype(np.float64)
           * theta[None, :].astype(np.float32).astype(np.float64))
    ang = ang.astype(np.float32).astype(np.float64)
    c["c_cos"] = np.cos(ang).astype(np.float32)
    c["c_sin"] = np.sin(ang).astype(np.float32)
    return c


def declare_constants(p):
    nc = p.nc
    g = {}
    for nm, shape, dt in [("c_ident", (128, 128), BF16),
                          ("c_ones1", (1, 128), BF16),
                          ("c_mask", (128, 128), F32),
                          ("c_cos", (T, 64), F32),
                          ("c_sin", (T, 64), F32)]:
        g[nm] = Buf(nm, nc.dram_tensor(nm, list(shape), dt,
                                       kind="ExternalInput"))
    sb = {}
    sb["ident"] = p.sbuf("ident", [128, 128], BF16)
    sb["ones1"] = p.sbuf("ones1", [1, 128], BF16)
    sb["mask"] = p.sbuf("mask", [128, 128], F32)
    dma(p, "sp", sb["ident"], sb["ident"].ap, g["c_ident"], g["c_ident"][:, :])
    dma(p, "sp", sb["ones1"], sb["ones1"].ap, g["c_ones1"], g["c_ones1"][:, :])
    dma(p, "sp", sb["mask"], sb["mask"].ap, g["c_mask"], g["c_mask"][:, :])
    sb["_ccos_d"] = g["c_cos"]
    sb["_csin_d"] = g["c_sin"]
    return sb


SCALE0 = float(128 ** -0.5)


def declare_l0_inputs(nc):
    g = {}
    def di(name, shape):
        g[name] = Buf(name, nc.dram_tensor(name, list(shape), F32,
                                           kind="ExternalInput"))
    di("L0_attn_x", (T, D))
    di("L0_attn_Wq", (2, 128, D))
    di("L0_attn_Wk", (128, D))
    di("L0_attn_Wv", (128, D))
    di("L0_attn_Wo", (D, 256))
    di("L0_ffn_router_w", (8, D))
    di("L0_ffn_w1_w", (8, 4096, D))
    di("L0_ffn_w1_b", (8, 32, 128))
    di("L0_ffn_w2_w", (8, D, 4096))
    di("L0_ffn_w2_b", (8, D))
    di("L0_ffn_wg_w", (8, 4096, D))
    di("L0_ffn_wg_b", (8, 32, 128))
    return g


def build_l0(p, g, cst, x1_out, dbg=(), nw_limit=None):
    nc = p.nc
    W = 512
    NW = T // W
    if nw_limit:
        NW = nw_limit
    NT = 4

    wq_bf = p.dram("wq_bf", [2, 128, D], BF16)
    wk_bf = p.dram("wk_bf", [128, D], BF16)
    wv_bf = p.dram("wv_bf", [128, D], BF16)
    wo_bf = p.dram("wo_bf", [D, 256], BF16)
    wg_bf = p.dram("wg_bf", [8, 4096, D], BF16)
    w1_bf = p.dram("w1_bf", [8, 4096, D], BF16)
    w2_bf = p.dram("w2_bf", [8, D, 4096], BF16)

    MD = dict(max_dma_last_dim=8192)
    for h in range(2):
        dma(p, "pool", wq_bf, wq_bf[h, :, :], g["L0_attn_Wq"],
            g["L0_attn_Wq"][h, :, :], tag="precast", **MD)
    dma(p, "pool", wk_bf, wk_bf[:, :], g["L0_attn_Wk"], g["L0_attn_Wk"][:, :],
        tag="precast", **MD)
    dma(p, "pool", wv_bf, wv_bf[:, :], g["L0_attn_Wv"], g["L0_attn_Wv"][:, :],
        tag="precast", **MD)
    dma(p, "pool", wo_bf, wo_bf[:, :], g["L0_attn_Wo"], g["L0_attn_Wo"][:, :],
        tag="precast", **MD)
    ptr_bf = [p.psum(f"ptr{i}", [128, 1024], BF16) for i in range(2)]
    WqT = p.sbuf("WqT", [128, 2 * 1024], BF16)
    WkT = p.sbuf("WkT", [128, 1024], BF16)
    WvT = p.sbuf("WvT", [128, 1024], BF16)
    WoT = p.sbuf("WoT", [128, 2 * 1024], BF16)
    for h in range(2):
        for dc in range(8):
            dma_tr(p, WqT, WqT[:, (h * 8 + dc) * 128:(h * 8 + dc) * 128 + 128],
                   wq_bf, wq_bf[h, 0:128, 128 * dc:128 * dc + 128])
    for dc in range(8):
        dma_tr(p, WkT, WkT[:, 128 * dc:128 * dc + 128],
               wk_bf, wk_bf[0:128, 128 * dc:128 * dc + 128])
        dma_tr(p, WvT, WvT[:, 128 * dc:128 * dc + 128],
               wv_bf, wv_bf[0:128, 128 * dc:128 * dc + 128])
    for h in range(2):
        dma_tr(p, WoT, WoT[:, 1024 * h:1024 * h + 1024],
               wo_bf, wo_bf[:, 128 * h:128 * h + 128])
    idf0 = p.sbuf("ident_f32_l0", [128, 128], F32)
    copy(p, "dve", idf0, idf0.ap, cst["ident"], cst["ident"].ap)
    cst["ident_f32"] = idf0
    router_sb = p.sbuf("router_sb", [8, 1024], F32)
    dma(p, "sp", router_sb, router_sb.h[0:8, :], g["L0_ffn_router_w"],
        g["L0_ffn_router_w"][:, :])
    routerT = p.sbuf("routerT", [128, 64], F32)
    w1b_cols = p.sbuf("w1b_cols", [128, 8 * 32], BF16)
    wgb_cols = p.sbuf("wgb_cols", [128, 8 * 32], BF16)
    bias_nat = p.sbuf("bias_nat", [32, 128], BF16)
    for e in range(8):
        for (srcb, dstb) in ((g["L0_ffn_w1_b"], w1b_cols),
                             (g["L0_ffn_wg_b"], wgb_cols)):
            dma(p, "pool", bias_nat, bias_nat.h[0:32, :], srcb,
                srcb[e, :, :])
            pt = ptr_bf[0]
            transpose(p, pt, pt.h[:, 0:32], bias_nat,
                      bias_nat.h[0:32, :], cst["ident"], kdim=32)
            copy(p, "dve", dstb, dstb[:, 32 * e:32 * e + 32], pt,
                 pt.h[:, 0:32])
    w2b_row = [p.sbuf(f"w2b_row{i}", [1, 1024], BF16) for i in range(2)]

    xn0_fm = p.sbuf("xn0_fm", [128, 8 * W], BF16)
    acc = p.sbuf("acc_moe", [128, NT * 1024], F32)
    h_buf = p.sbuf("h_buf", [128, 16 * W], BF16)
    wgT_sl = [p.sbuf(f"wgT_sl{i}", [128, 8 * 256], BF16) for i in range(2)]
    w1T_sl = [p.sbuf(f"w1T_sl{i}", [128, 8 * 256], BF16) for i in range(2)]
    w2T_buf = p.sbuf("w2T_buf", [128, 16 * 1024], BF16)
    K_fm = p.sbuf("K_fm", [128, T], BF16)
    V_ext = p.sbuf("V_ext", [128, 32 * 132], BF16)
    memset(p, "dve", V_ext, V_ext.ap, 1.0)
    scores = p.sbuf("scores", [128, T], F32)
    Pb = [p.sbuf(f"Pb{i}", [128, 512], BF16) for i in range(2)]
    PTb = [p.sbuf(f"PTb{i}", [128, 128], BF16) for i in range(2)]
    ropebuf = p.sbuf("ropebuf", [128, 2 * 64], F32)
    ropebuf2 = p.sbuf("ropebuf2", [128, 2 * 64], F32)
    krope = [p.sbuf(f"krope{i}", [128, 128], BF16) for i in range(2)]
    qfm = p.sbuf("qfm", [128, 128], BF16)
    attn_fm = p.sbuf("attn_fm", [128, 256], BF16)
    atm = p.sbuf("atm", [128, 128], BF16)
    costl = p.sbuf("costl", [128, NT * 64], F32)
    sintl = p.sbuf("sintl", [128, NT * 64], F32)
    xtile = [p.sbuf(f"xtile{i}", [128, 1024], F32) for i in range(2)]
    junk = p.sbuf("junk", [128, 1024], BF16)
    xn32 = p.sbuf("xn32", [128, 1024], F32)
    xnf32_fm = p.sbuf("xnf32_fm", [128, 1024], F32)
    gates = p.sbuf("gates", [128, 32 * 8], F32)
    gbuf = [p.sbuf(f"gbuf{i}", [128, 512], BF16) for i in range(2)]
    g0buf = p.sbuf("g0buf", [128, 1024], F32)
    prodbuf2 = [p.sbuf(f"prodbuf{i}", [128, 1024], F32) for i in range(2)]
    stat = p.sbuf("stat", [128, 16], F32)
    probs_u = p.sbuf("probs_u", [128, 8], F32)
    probs = p.sbuf("probs", [128, 8], F32)
    mask1 = p.sbuf("mask1", [128, 8], F32)
    pmm = [p.psum(f"pmm{i}", [128, 512], F32) for i in range(6)]
    memset(p, "dve", stat, stat[:, 14:15], 1e-5)
    for dc in range(8):
        transpose(p, pmm[dc % 2], pmm[dc % 2].h[:, 0:8], router_sb,
                  router_sb.h[0:8, 128 * dc:128 * dc + 128], cst["ident_f32"],
                  kdim=8)
        copy(p, "dve", routerT, routerT[:, 8 * dc:8 * dc + 8],
             pmm[dc % 2], pmm[dc % 2].h[:, 0:8])
    wgT_d2 = [p.dram(f"wgT_d{e}", [16, 128, 8, 256], BF16) for e in range(8)]
    w1T_d2 = [p.dram(f"w1T_d{e}", [16, 128, 8, 256], BF16) for e in range(8)]
    w2T_d = [p.dram(f"w2T_d{e}", [4096, 1024], BF16) for e in range(8)]
    stg = [p.sbuf(f"stg{i}", [128, 4096], BF16) for i in range(3)]
    stg_k = [0]

    def stage_expert(e):
        dma(p, "pool", wg_bf, wg_bf[e, :, :], g["L0_ffn_wg_w"],
            g["L0_ffn_wg_w"][e, :, :], tag="precast", **MD)
        dma(p, "pool", w1_bf, w1_bf[e, :, :], g["L0_ffn_w1_w"],
            g["L0_ffn_w1_w"][e, :, :], tag="precast", **MD)
        dma(p, "pool", w2_bf, w2_bf[e, :, :], g["L0_ffn_w2_w"],
            g["L0_ffn_w2_w"][e, :, :], tag="precast", **MD)
        for (srcb, dstd) in ((wg_bf, wgT_d2[e]), (w1_bf, w1T_d2[e])):
            for dc in range(8):
                sb = stg[stg_k[0] % 3]; stg_k[0] += 1
                dma_tr(p, sb, sb.ap, srcb,
                       srcb[e, :, 128 * dc:128 * dc + 128])
                dma(p, "pool", dstd,
                    dstd[:, :, dc, :].rearrange("fs q f -> q fs f"),
                    sb, sb.ap.rearrange("q (fs f) -> q fs f", f=256),
                    tag="wtstore")
        for fq in range(8):
            sb = stg[stg_k[0] % 3]; stg_k[0] += 1
            for j in range(4):
                fc = 4 * fq + j
                dma_tr(p, sb, sb[:, 1024 * j:1024 * j + 1024],
                       w2_bf, w2_bf[e, :, 128 * fc:128 * fc + 128])
            dma(p, "pool", w2T_d[e],
                w2T_d[e][512 * fq:512 * fq + 512, :].rearrange(
                    "(c q) d -> q c d", q=128),
                sb, sb.ap.rearrange("q (c d) -> q c d", c=4), tag="wtstore")

    S_SUM, S_SQ, S_NM, S_RS, S_VAR, S_M8, S_NM8, S_S8, S_RI, S_M1, S_M2, \
        S_MX, S_NB, S_LI, S_EPS = range(15)

    def col(idx):
        return stat[:, idx:idx + 1]

    dbg_bufs = {}
    def dump(key, buf, ap, shape, dt=F32):
        if key in dbg:
            b = Buf(key, nc.dram_tensor("dbg_" + key, list(shape), dt,
                                        kind="ExternalOutput"))
            dbg_bufs[key] = b
            dma(p, "sp", b, b[:, :], buf, ap, tag="dbg")

    for w in range(NW):
        i0 = NT * w
        for il in range(NT):
            i = i0 + il
            dma(p, "sp", costl, costl[:, 64 * il:64 * il + 64], cst["_ccos_d"],
                cst["_ccos_d"][128 * i:128 * i + 128, :])
            dma(p, "sp", sintl, sintl[:, 64 * il:64 * il + 64], cst["_csin_d"],
                cst["_csin_d"][128 * i:128 * i + 128, :])
        for il in range(NT):
            i = i0 + il
            xt = xtile[il % 2]
            dma(p, "sp", xt, xt.ap, g["L0_attn_x"],
                g["L0_attn_x"][128 * i:128 * i + 128, :], tag="xload")
            act(p, junk, junk.ap, xt, xt.ap, AF.Square,
                accum=stat, accum_ap=col(S_SQ), tag="ln0sq")
            reduce(p, stat, col(S_SUM), xt, xt.ap, ALU.add, tag="ln0sum")
            ts(p, "dve", stat, col(S_NM), stat, col(S_SUM), -1.0 / D, ALU.mult)
            ts(p, "dve", stat, col(S_VAR), stat, col(S_NM), col(S_NM), ALU.mult)
            stt(p, "dve", stat, col(S_VAR), stat, col(S_SQ), 1.0 / D,
                stat, col(S_VAR), ALU.mult, ALU.subtract)
            act(p, stat, col(S_RS), stat, col(S_VAR), AF.Ln, bias=col(S_EPS))
            act(p, stat, col(S_RS), stat, col(S_RS), AF.Exp, scale=-0.5)
            ts(p, "dve", xn32, xn32.ap, xt, xt.ap, col(S_NM), ALU.add,
               col(S_RS), ALU.mult, extra_reads=(stat,), tag="ln0apply")
            xn_tm = junk
            copy(p, "dve", xn_tm, xn_tm.ap, xn32, xn32.ap)
            for dc in range(8):
                pt = ptr_bf[dc % 2]
                transpose(p, pt, pt.h[:, 0:128], xn_tm,
                          xn_tm[:, 128 * dc:128 * dc + 128], cst["ident"])
                copy(p, "dve", xn0_fm,
                     xn0_fm[:, dc * W + 128 * il:dc * W + 128 * il + 128],
                     pt, pt.h[:, 0:128])
            pk = pmm[il % 2]
            for dc in range(8):
                mm(p, pk, pk.h[:, 0:128],
                   xn0_fm, xn0_fm[:, dc * W + 128 * il:dc * W + 128 * il + 128],
                   WkT, WkT[:, 128 * dc:128 * dc + 128],
                   start=(dc == 0), stop=(dc == 7), tag="kproj")
            _rope(p, pk, krope[0], costl, sintl, il, ropebuf, ropebuf2)
            pt = ptr_bf[0]
            transpose(p, pt, pt.h[:, 0:128], krope[0], krope[0].ap,
                      cst["ident"])
            copy(p, "dve", K_fm, K_fm[:, 128 * i:128 * i + 128], pt,
                 pt.h[:, 0:128])
            pv = pmm[il % 2]
            for dc in range(8):
                mm(p, pv, pv.h[:, 0:128],
                   xn0_fm, xn0_fm[:, dc * W + 128 * il:dc * W + 128 * il + 128],
                   WvT, WvT[:, 128 * dc:128 * dc + 128],
                   start=(dc == 0), stop=(dc == 7), tag="vproj")
            copy(p, "dve", V_ext, V_ext[:, 132 * i:132 * i + 128], pv,
                 pv.h[:, 0:128])
            for dc in range(8):
                pz = pmm[dc % 2]
                transpose(p, pz, pz.h[:, 0:128], xn32,
                          xn32[:, 128 * dc:128 * dc + 128], cst["ident_f32"])
                copy(p, "dve", xnf32_fm,
                     xnf32_fm[:, 128 * dc:128 * dc + 128], pz, pz.h[:, 0:128])
            pr = pmm[il % 2]
            for dc in range(8):
                mm(p, pr, pr.h[:, 0:8],
                   xnf32_fm, xnf32_fm[:, 128 * dc:128 * dc + 128],
                   routerT, routerT[:, 8 * dc:8 * dc + 8],
                   start=(dc == 0), stop=(dc == 7), tag="router")
            reduce(p, stat, col(S_M8), pr, pr.h[:, 0:8], ALU.max)
            ts(p, "dve", stat, col(S_NM8), stat, col(S_M8), -1.0, ALU.mult)
            act(p, probs_u, probs_u.ap, pr, pr.h[:, 0:8], AF.Exp,
                bias=col(S_NM8), extra_reads=(stat,))
            reduce(p, stat, col(S_S8), probs_u, probs_u.ap, ALU.add)
            recip(p, stat, col(S_RI), stat, col(S_S8))
            ts(p, "dve", probs, probs.ap, probs_u, probs_u.ap, col(S_RI),
               ALU.mult, extra_reads=(stat,))
            reduce(p, stat, col(S_M1), probs, probs.ap, ALU.max)
            ts(p, "dve", mask1, mask1.ap, probs, probs.ap, col(S_M1),
               ALU.is_ge, extra_reads=(stat,))
            ts(p, "dve", mask1, mask1.ap, mask1, mask1.ap, -1.0, ALU.mult,
               1.0, ALU.add)
            tt(p, "dve", probs_u, probs_u.ap, probs, probs.ap, mask1,
               mask1.ap, ALU.mult)
            reduce(p, stat, col(S_M2), probs_u, probs_u.ap, ALU.max)
            ts(p, "dve", mask1, mask1.ap, probs, probs.ap, col(S_M2),
               ALU.is_ge, extra_reads=(stat,))
            tt(p, "dve", gates, gates[:, 8 * i:8 * i + 8], probs, probs.ap,
               mask1, mask1.ap, ALU.mult)
        if w == 0:
            dump("xn0", xn0_fm, xn0_fm.ap, (128, 8 * 512), BF16)
            dump("gates0", gates, gates[:, 0:32], (128, 32))
        def emit_slab(idx, s):
            e, fh = divmod(idx, 2)
            fs = fh * 8 + s
            dma(p, "sp", wgT_sl[s % 2],
                wgT_sl[s % 2].ap.rearrange("q (dc f) -> q dc f", dc=8),
                wgT_d2[e], wgT_d2[e][fs, :, :, :], tag="wgsl")
            dma(p, "act", w1T_sl[s % 2],
                w1T_sl[s % 2].ap.rearrange("q (dc f) -> q dc f", dc=8),
                w1T_d2[e], w1T_d2[e][fs, :, :, :], tag="w1sl")
        if w == 0:
            stage_expert(0)
        emit_slab(0, 0)
        emit_slab(0, 1)
        for idx in range(16):
            e, fh = divmod(idx, 2)
            if fh == 1 and w == 0 and e + 1 < 8:
                stage_expert(e + 1)
            if fh == 0:
                w2br = w2b_row[e % 2]
                dma(p, "pool", w2br, w2br.h[0:1, :], g["L0_ffn_w2_b"],
                    g["L0_ffn_w2_b"][e:e + 1, :])
            else:
                w2br = w2b_row[e % 2]
            for q4 in range(4):
                f0 = 512 * (4 * fh + q4)
                dma(p, "pool", w2T_buf,
                    w2T_buf[:, 4096 * q4:4096 * q4 + 4096].rearrange(
                        "q (c d) -> q c d", c=4),
                    w2T_d[e], w2T_d[e][f0:f0 + 512, :].rearrange(
                        "(c q) d -> q c d", q=128), tag="w2sl")
            for s in range(8):
                wsl = wgT_sl[s % 2]
                usl = w1T_sl[s % 2]
                fs = fh * 8 + s
                for fl in range(2):
                    fc = fs * 2 + fl
                    fi = s * 2 + fl
                    pG = pmm[fl % 2]
                    pU = pmm[2 + fl % 2]
                    for dc in range(8):
                        mm(p, pG, pG.ap, wsl,
                           wsl[:, 256 * dc + 128 * fl:256 * dc + 128 * fl + 128],
                           xn0_fm, xn0_fm[:, dc * W:dc * W + W],
                           start=(dc == 0), stop=(dc == 7), tag="moeG")
                    for dc in range(8):
                        mm(p, pU, pU.ap, usl,
                           usl[:, 256 * dc + 128 * fl:256 * dc + 128 * fl + 128],
                           xn0_fm, xn0_fm[:, dc * W:dc * W + W],
                           start=(dc == 0), stop=(dc == 7), tag="moeU")
                    gb = gbuf[fl % 2]
                    act(p, gb, gb.ap, pG, pG.ap, AF.Gelu,
                        bias=wgb_cols[:, 32 * e + fc:32 * e + fc + 1],
                        extra_reads=(wgb_cols,), tag="gelu")
                    stt(p, "dve", h_buf, h_buf[:, fi * W:fi * W + W],
                        pU, pU.ap, w1b_cols[:, 32 * e + fc:32 * e + fc + 1],
                        gb, gb.ap, ALU.add, ALU.mult,
                        extra_reads=(w1b_cols,), tag="hmoe")
                if s + 2 < 8:
                    emit_slab(idx, s + 2)
                elif idx + 1 < 16:
                    emit_slab(idx + 1, s + 2 - 8)
            for tsub in range(NT):
                i = i0 + tsub
                gcol = gates[:, 8 * i + e:8 * i + e + 1]
                for half in range(2):
                    po = pmm[4 + half]
                    first = True
                    if fh == 0:
                        mm(p, po, po.ap, cst["ones1"], cst["ones1"].ap,
                           w2br, w2br.h[0:1, 512 * half:512 * half + 512],
                           start=True, stop=False, tag="w2bias")
                        first = False
                    for fi in range(16):
                        mm(p, po, po.ap, h_buf,
                           h_buf[:, fi * W + 128 * tsub:fi * W + 128 * tsub + 128],
                           w2T_buf,
                           w2T_buf[:, 1024 * fi + 512 * half:
                                   1024 * fi + 512 * half + 512],
                           start=(first and fi == 0),
                           stop=(fi == 15), tag="moeO")
                    aap = acc[:, 1024 * tsub + 512 * half:
                              1024 * tsub + 512 * half + 512]
                    if e == 0 and fh == 0:
                        ts(p, "dve", acc, aap, po, po.ap, gcol, ALU.mult,
                           extra_reads=(gates,), tag="accw")
                    else:
                        stt(p, "dve", acc, aap, po, po.ap, gcol,
                            acc, aap, ALU.mult, ALU.add,
                            extra_reads=(gates,), tag="acca")
        for il in range(NT):
            i = i0 + il
            slen = 128 * (i + 1)
            for h in range(2):
                pq = pmm[0]
                for dc in range(8):
                    mm(p, pq, pq.h[:, 0:128],
                       xn0_fm, xn0_fm[:, dc * W + 128 * il:dc * W + 128 * il + 128],
                       WqT, WqT[:, (h * 8 + dc) * 128:(h * 8 + dc) * 128 + 128],
                       start=(dc == 0), stop=(dc == 7), tag="qproj")
                _rope(p, pq, krope[1], costl, sintl, il, ropebuf, ropebuf2)
                pt = ptr_bf[1]
                transpose(p, pt, pt.h[:, 0:128], krope[1], krope[1].ap,
                          cst["ident"])
                copy(p, "dve", qfm, qfm.ap, pt, pt.h[:, 0:128])
                nch = (slen + 511) // 512
                for sc in range(nch):
                    wdt = min(512, slen - 512 * sc)
                    ps = pmm[sc % 2]
                    mm(p, ps, ps.h[:, 0:wdt], qfm, qfm.ap,
                       K_fm, K_fm[:, 512 * sc:512 * sc + wdt],
                       start=True, stop=True, tag="scores")
                    last = (sc == nch - 1)
                    if last:
                        if wdt > 128:
                            copy(p, "act", scores,
                                 scores[:, 512 * sc:512 * sc + wdt - 128],
                                 ps, ps.h[:, 0:wdt - 128])
                        tt(p, "dve", scores, scores[:, slen - 128:slen],
                           ps, ps.h[:, wdt - 128:wdt], cst["mask"],
                           cst["mask"].ap, ALU.add)
                    else:
                        copy(p, "act", scores,
                             scores[:, 512 * sc:512 * sc + wdt],
                             ps, ps.h[:, 0:wdt])
                reduce(p, stat, col(S_MX), scores, scores[:, 0:slen], ALU.max)
                ts(p, "dve", stat, col(S_NB), stat, col(S_MX), -SCALE0,
                   ALU.mult)
                pa = pmm[4]
                for sc in range(nch):
                    wdt = min(512, slen - 512 * sc)
                    pb = Pb[sc % 2]
                    act(p, pb, pb.h[:, 0:wdt], scores,
                        scores[:, 512 * sc:512 * sc + wdt], AF.Exp,
                        bias=col(S_NB), scale=SCALE0, extra_reads=(stat,),
                        tag="pexp")
                    for jj in range(wdt // 128):
                        j = sc * 4 + jj
                        pt2 = ptr_bf[j % 2]
                        transpose(p, pt2, pt2.h[:, 0:128], pb,
                                  pb.h[:, 128 * jj:128 * jj + 128],
                                  cst["ident"])
                        ptsb = PTb[j % 2]
                        copy(p, "dve", ptsb, ptsb.ap, pt2, pt2.h[:, 0:128])
                        mm(p, pa, pa.h[:, 0:129], ptsb, ptsb.ap,
                           V_ext, V_ext[:, 132 * j:132 * j + 129],
                           start=(j == 0), stop=(j == i), tag="pv")
                recip(p, stat, col(S_LI), pa, pa.h[:, 128:129])
                ts(p, "dve", atm, atm.ap, pa, pa.h[:, 0:128], col(S_LI),
                   ALU.mult, extra_reads=(stat,))
                pt3 = ptr_bf[h % 2]
                transpose(p, pt3, pt3.h[:, 0:128], atm, atm.ap, cst["ident"])
                copy(p, "dve", attn_fm, attn_fm[:, 128 * h:128 * h + 128],
                     pt3, pt3.h[:, 0:128])
            po_s = [pmm[2 + il % 2], pmm[4 + il % 2]]
            for half in range(2):
                for h in range(2):
                    mm(p, po_s[half], po_s[half].ap,
                       attn_fm, attn_fm[:, 128 * h:128 * h + 128],
                       WoT, WoT[:, 1024 * h + 512 * half:1024 * h + 512 * half + 512],
                       start=(h == 0), stop=(h == 1), tag="wo")
            act(p, g0buf, g0buf.ap, acc, acc[:, 1024 * il:1024 * il + 1024],
                AF.Silu, tag="silu0")
            xt2 = xtile[il % 2]
            dma(p, "sp", xt2, xt2.ap, g["L0_attn_x"],
                g["L0_attn_x"][128 * i:128 * i + 128, :], tag="xload2")
            pb2 = prodbuf2[il % 2]
            for half in range(2):
                tt(p, "dve", pb2, pb2.h[:, 512 * half:512 * half + 512],
                   po_s[half], po_s[half].ap, g0buf,
                   g0buf.h[:, 512 * half:512 * half + 512], ALU.mult)
            tt(p, "pool", pb2, pb2.ap, pb2, pb2.ap, xt2, xt2.ap, ALU.add)
            dma(p, "sp", x1_out, x1_out[128 * i:128 * i + 128, :], pb2,
                pb2.ap, tag="x1store")
    dump("Kfm", K_fm, K_fm.ap, (128, T), BF16)
    dump("Vx", V_ext, V_ext.ap, (128, 32 * 132), BF16)
    return dbg_bufs


def _rope(p, psrc, dst_bf, costl, sintl, il, rb, rb2):
    ce = costl[:, 64 * il:64 * il + 64]
    se = sintl[:, 64 * il:64 * il + 64]
    ev = psrc.h[:, 0:128:2]
    od = psrc.h[:, 1:128:2]
    tt(p, "dve", rb, rb.h[:, 0:64], psrc, ev, costl, ce, ALU.mult)
    tt(p, "dve", rb, rb.h[:, 64:128], psrc, od, sintl, se, ALU.mult)
    tt(p, "dve", dst_bf, dst_bf.h[:, 0:128:2], rb, rb.h[:, 0:64], rb,
       rb.h[:, 64:128], ALU.subtract)
    tt(p, "dve", rb2, rb2.h[:, 0:64], psrc, od, costl, ce, ALU.mult)
    tt(p, "dve", rb2, rb2.h[:, 64:128], psrc, ev, sintl, se, ALU.mult)
    tt(p, "dve", dst_bf, dst_bf.h[:, 1:128:2], rb2, rb2.h[:, 0:64], rb2,
       rb2.h[:, 64:128], ALU.add)



def make_l0_program(dbg=(), nw_limit=None):
    nc = bass.Bass("TRN2", target_bir_lowering=False, debug=False)
    p = Prog(nc)
    cst = declare_constants(p)
    g = declare_l0_inputs(nc)
    x1 = Buf("x1", nc.dram_tensor("x1", [T, D], F32, kind="ExternalOutput"))
    build_l0(p, g, cst, x1, dbg=dbg, nw_limit=nw_limit)
    p.emit(verbose=True)
    return nc


def l0_input_map(inputs, core):
    c = host_constants()
    m = dict(c)
    m["L0_attn_x"] = np.ascontiguousarray(inputs["L0_attn_x"][core])
    for k in ("L0_attn_Wq", "L0_attn_Wk", "L0_attn_Wv", "L0_attn_Wo",
              "L0_ffn_router_w", "L0_ffn_w1_w", "L0_ffn_w1_b", "L0_ffn_w2_w",
              "L0_ffn_w2_b", "L0_ffn_wg_w", "L0_ffn_wg_b"):
        v = np.ascontiguousarray(inputs[k])
        if k in ("L0_ffn_w1_b", "L0_ffn_wg_b"):
            v = v.reshape(8, 32, 128)
        m[k] = v
    return m



def declare_l1_inputs(nc):
    g = {}
    def di(name, shape):
        g[name] = Buf(name, nc.dram_tensor(name, list(shape), F32,
                                           kind="ExternalInput"))
    di("L1_attn_conv_w", (D, 7))
    di("L1_attn_norm1_w", (8, 128))
    di("L1_attn_norm2_w", (8, 128))
    di("L1_attn_Wdecay", (D, D))
    di("L1_attn_bdecay", (8, 128))
    di("L1_attn_Wgate", (D, D))
    di("L1_attn_bgate", (8, 128))
    di("L1_attn_state_scale", (1, 1))
    di("L1_attn_Wk", (D, 256))
    di("L1_attn_M", (256, 256))
    di("L1_attn_Wmem", (256, D))
    di("L1_attn_mem_scale", (1, 1))
    di("L1_attn_Wc", (3 * D, D))
    di("L1_attn_bc", (1, D))
    di("L1_attn_Wup", (D, 4096))
    di("L1_attn_Wg", (D, 4096))
    di("L1_attn_Wdown", (4096, D))
    di("L1_ffn_weight", (D, D))
    di("L1_ffn_aux_weight1", (D, D))
    di("L1_ffn_aux_weight2", (D, D))
    di("L1_ffn_bias", (1, D))
    return g


def _vec_cols_f32(p, name, src_buf, src_ap_2d, nchunk, cst, pf32):
    nat = p.sbuf(name + "_nat", [max(nchunk, 1), 128], F32)
    out = p.sbuf(name, [128, nchunk], F32)
    dma(p, "sp", nat, nat.h[0:nchunk, :], src_buf, src_ap_2d)
    idf = cst["ident_f32"]
    transpose(p, pf32, pf32.h[:, 0:nchunk], nat, nat.h[0:nchunk, :],
              idf, kdim=nchunk)
    copy(p, "dve", out, out[:, 0:nchunk], pf32, pf32.h[:, 0:nchunk])
    return out


def build_l1(p, g, cst, x1_in, out_d, dbg=()):
    nc = p.nc
    dbg_bufs = {}
    def dump(key, buf, ap, shape, dt=F32):
        if key in dbg:
            b = Buf(key, nc.dram_tensor("dbg_" + key, list(shape), dt,
                                        kind="ExternalOutput"))
            dbg_bufs[key] = b
            dma(p, "sp", b, b[:, :], buf, ap, tag="dbg")
    TW = 512
    NTW = T // TW
    NT = 4

    pw = [p.psum(f"l1pw{i}", [128, 1024], F32) for i in range(2)]
    pmms = [p.psum(f"l1pmm{i}", [128, 512], F32) for i in range(2)]
    pmm = pmms[0]
    ptrs = [p.psum(f"l1ptr{i}", [128, 1024], BF16) for i in range(2)]
    ptr = ptrs[0]

    MD = dict(max_dma_last_dim=8192)
    def precast(nm, shape):
        b = p.dram(nm + "_bf", list(shape), BF16)
        dma(p, "pool", b, b[:, :], g[nm], g[nm][:, :], tag="precast", **MD)
        return b
    wdec_bf = precast("L1_attn_Wdecay", (D, D))
    wgat_bf = precast("L1_attn_Wgate", (D, D))
    wk1_bf = precast("L1_attn_Wk", (D, 256))
    m_bf = precast("L1_attn_M", (256, 256))
    wmem_bf = precast("L1_attn_Wmem", (256, D))
    wc_bf = precast("L1_attn_Wc", (3 * D, D))
    wup_bf = precast("L1_attn_Wup", (D, 4096))
    wgg_bf = precast("L1_attn_Wg", (D, 4096))
    wdn_bf = precast("L1_attn_Wdown", (4096, D))
    ffw_bf = precast("L1_ffn_weight", (D, D))
    fa1_bf = precast("L1_ffn_aux_weight1", (D, D))
    fa2_bf = precast("L1_ffn_aux_weight2", (D, D))

    wk_sb = p.sbuf("wk_sb", [128, 8 * 256], BF16)
    for dc in range(8):
        dma(p, "sp", wk_sb, wk_sb[:, 256 * dc:256 * dc + 256],
            wk1_bf, wk1_bf[128 * dc:128 * dc + 128, :])
    MT_sb = p.sbuf("MT_sb", [128, 2 * 256], BF16)
    for ic in range(2):
        dma_tr(p, MT_sb, MT_sb[:, 256 * ic:256 * ic + 256],
               m_bf, m_bf[0:256, 128 * ic:128 * ic + 128])
    wmem_sb = p.sbuf("wmem_sb", [128, 2 * 1024], BF16)
    for jc in range(2):
        dma(p, "sp", wmem_sb, wmem_sb[:, 1024 * jc:1024 * jc + 1024],
            wmem_bf, wmem_bf[128 * jc:128 * jc + 128, :])
    idf = p.sbuf("ident_f32", [128, 128], F32)
    copy(p, "dve", idf, idf.ap, cst["ident"], cst["ident"].ap)
    cst["ident_f32"] = idf
    n1w_cols = _vec_cols_f32(p, "n1w", g["L1_attn_norm1_w"],
                             g["L1_attn_norm1_w"][:, :], 8, cst, pmms[1])
    n2w_cols = _vec_cols_f32(p, "n2w", g["L1_attn_norm2_w"],
                             g["L1_attn_norm2_w"][:, :], 8, cst, pmms[1])
    bdec_cols = _vec_cols_f32(p, "bdec", g["L1_attn_bdecay"],
                              g["L1_attn_bdecay"][:, :], 8, cst, pmms[1])
    bgat_cols = _vec_cols_f32(p, "bgat", g["L1_attn_bgate"],
                              g["L1_attn_bgate"][:, :], 8, cst, pmms[1])
    conv_sb = p.sbuf("conv_sb", [128, 8 * 7], F32)
    for dc in range(8):
        dma(p, "sp", conv_sb, conv_sb[:, 7 * dc:7 * dc + 7],
            g["L1_attn_conv_w"], g["L1_attn_conv_w"][128 * dc:128 * dc + 128, :])
    bc_row = p.sbuf("bc_row", [1, 1024], BF16)
    dma(p, "pool", bc_row, bc_row.h[0:1, :], g["L1_attn_bc"],
        g["L1_attn_bc"][:, :])
    ffb_row = p.sbuf("ffb_row", [1, 1024], BF16)
    dma(p, "pool", ffb_row, ffb_row.h[0:1, :], g["L1_ffn_bias"],
        g["L1_ffn_bias"][:, :])
    ones_f32 = p.sbuf("ones_f32", [1, 128], F32)
    copy(p, "dve", ones_f32, ones_f32.h[0:1, :], cst["ones1"],
         cst["ones1"].h[0:1, :])
    sc_in = p.sbuf("sc_in", [1, 2], F32)
    dma(p, "sp", sc_in, sc_in.h[0:1, 0:1], g["L1_attn_state_scale"],
        g["L1_attn_state_scale"][:, :])
    dma(p, "sp", sc_in, sc_in.h[0:1, 1:2], g["L1_attn_mem_scale"],
        g["L1_attn_mem_scale"][:, :])
    sc_cols = p.sbuf("sc_cols", [128, 2], F32)
    mm(p, pmm, pmm.h[:, 0:2], ones_f32, ones_f32.h[0:1, :], sc_in,
       sc_in.h[0:1, 0:2], start=True, stop=True, tag="scbcast")
    copy(p, "dve", sc_cols, sc_cols.ap, pmm, pmm.h[:, 0:2])
    ss_col = sc_cols[:, 0:1]
    ms_col = sc_cols[:, 1:2]

    slab = [p.sbuf(f"l1slab{i}", [128, 8 * 1024], BF16) for i in range(2)]
    rch = [p.sbuf(f"l1rch{i}", [128, 4 * 1024], BF16) for i in range(2)]
    gsl = [p.sbuf(f"l1gsl{i}", [128, 8 * 512], BF16) for i in range(2)]
    usl = [p.sbuf(f"l1usl{i}", [128, 8 * 512], BF16) for i in range(2)]

    x1t = [p.sbuf(f"x1t{i}", [128, 1024], F32) for i in range(2)]
    xn1_tm = p.sbuf("xn1_tm", [128, 4 * 1024], F32)
    xn1_fm = p.sbuf("xn1_fm", [128, 8 * TW], BF16)
    h_fm = p.sbuf("h_fm", [128, 8 * 518], BF16)
    cscr = p.sbuf("cscr", [128, 7 * 512], BF16)
    local_fm = p.sbuf("local_fm", [128, 8 * TW], BF16)
    state_fm = p.sbuf("state_fm", [128, 8 * TW], BF16)
    mem_fm = p.sbuf("mem_fm", [128, 8 * TW], BF16)
    k_fm = p.sbuf("k_fm", [128, 2 * TW], BF16)
    memo_fm = p.sbuf("memo_fm", [128, 2 * TW], BF16)
    dbuf = p.sbuf("dbuf", [128, TW], F32)
    gbuf2 = p.sbuf("gbuf2", [128, TW], F32)
    updb = p.sbuf("updb", [128, TW], F32)
    wrec = p.sbuf("wrec", [128, TW], F32)
    cumd = p.sbuf("cumd", [128, TW + 1], F32)
    cum2 = p.sbuf("cum2", [128, TW], F32)
    carry_cd = p.sbuf("carry_cd", [128, 8], F32)
    carry_c2 = p.sbuf("carry_c2", [128, 8], F32)
    h2_fm = state_fm
    ffin = mem_fm
    b_fm = local_fm
    silg = [p.sbuf(f"silg{i}", [128, TW], BF16) for i in range(2)]
    tb = [p.sbuf(f"tb{i}", [128, 1024], F32) for i in range(2)]
    gb = [p.sbuf(f"gb{i}", [128, 1024], F32) for i in range(2)]
    prodb = tb
    outb = gb
    junk1 = p.sbuf("junk1", [128, 1024], BF16)
    stat1 = p.sbuf("stat1", [128, 8], F32)
    L_SQ, L_RS, L_EPS1, L_EPS2 = range(4)

    def col(i):
        return stat1[:, i:i + 1]

    memset(p, "dve", stat1, col(L_EPS1), 1e-5)
    memset(p, "dve", stat1, col(L_EPS2), 1e-6)
    memset(p, "dve", carry_cd, carry_cd.ap, 1.0)
    memset(p, "dve", carry_c2, carry_c2.ap, 0.0)
    for dc in range(8):
        memset(p, "dve", h_fm, h_fm[:, 518 * dc:518 * dc + 6], 0.0)

    for it in range(NTW):
        t0 = TW * it
        for ts_ in range(NT):
            i = 4 * it + ts_
            xt = x1t[ts_ % 2]
            dma(p, "sp", xt, xt.ap, x1_in, x1_in[128 * i:128 * i + 128, :],
                tag="x1load")
            act(p, junk1, junk1.ap, xt, xt.ap, AF.Square,
                accum=stat1, accum_ap=col(L_SQ), tag="ln1sq")
            reduce(p, stat1, col(L_RS), xt, xt.ap, ALU.add)
            ts(p, "dve", stat1, col(L_RS), stat1, col(L_RS), 1.0 / D, ALU.mult)
            ts(p, "dve", stat1, col(L_SQ), stat1, col(L_SQ), 1.0 / D, ALU.mult)
            stt(p, "dve", stat1, col(L_SQ), stat1, col(L_RS), col(L_RS),
                stat1, col(L_SQ), ALU.mult, ALU.subtract)
            ts(p, "dve", stat1, col(L_SQ), stat1, col(L_SQ), -1.0, ALU.mult)
            act(p, stat1, col(L_SQ), stat1, col(L_SQ), AF.Ln, bias=col(L_EPS1))
            act(p, stat1, col(L_SQ), stat1, col(L_SQ), AF.Exp, scale=-0.5)
            ts(p, "dve", stat1, col(L_RS), stat1, col(L_RS), -1.0, ALU.mult)
            ts(p, "dve", xn1_tm, xn1_tm[:, 1024 * ts_:1024 * ts_ + 1024],
               xt, xt.ap, col(L_RS), ALU.add, col(L_SQ), ALU.mult,
               extra_reads=(stat1,), tag="ln1apply")
            copy(p, "dve", junk1, junk1.ap, xn1_tm,
                 xn1_tm[:, 1024 * ts_:1024 * ts_ + 1024])
            for dc in range(8):
                ptx = ptrs[dc % 2]
                transpose(p, ptx, ptx.h[:, 0:128], junk1,
                          junk1[:, 128 * dc:128 * dc + 128], cst["ident"])
                copy(p, "dve", xn1_fm,
                     xn1_fm[:, TW * dc + 128 * ts_:TW * dc + 128 * ts_ + 128],
                     ptx, ptx.h[:, 0:128])
        for ts_ in range(NT):
            i = 4 * it + ts_
            xns = xn1_tm[:, 1024 * ts_:1024 * ts_ + 1024]
            act(p, junk1, junk1.ap, xn1_tm, xns, AF.Square,
                accum=stat1, accum_ap=col(L_SQ), tag="rms1sq")
            ts(p, "dve", stat1, col(L_SQ), stat1, col(L_SQ), 1.0 / D, ALU.mult)
            act(p, stat1, col(L_SQ), stat1, col(L_SQ), AF.Ln, bias=col(L_EPS2))
            act(p, stat1, col(L_SQ), stat1, col(L_SQ), AF.Exp, scale=-0.5)
            ts(p, "dve", junk1, junk1.ap, xn1_tm, xns, col(L_SQ), ALU.mult,
               extra_reads=(stat1,), tag="rms1apply")
            if it == 0 and ts_ == 0:
                dump("rms1o", junk1, junk1.ap, (128, 1024), BF16)
                dump("n1wc", n1w_cols, n1w_cols.ap, (128, 8))
            for dc in range(8):
                ptx = ptrs[dc % 2]
                transpose(p, ptx, ptx.h[:, 0:128], junk1,
                          junk1[:, 128 * dc:128 * dc + 128], cst["ident"])
                ts(p, "dve", h_fm,
                   h_fm[:, 518 * dc + 6 + 128 * ts_:518 * dc + 6 + 128 * ts_ + 128],
                   ptx, ptx.h[:, 0:128], n1w_cols[:, dc:dc + 1], ALU.mult,
                   extra_reads=(n1w_cols,), tag="hevac")
        if it == 0:
            dump("xn1fm", xn1_fm, xn1_fm.ap, (128, 4096), BF16)
            dump("h", h_fm, h_fm.ap, (128, 8 * 518), BF16)
        dma(p, "sp", slab[0], slab[0].ap.rearrange("q (dc o) -> q dc o", dc=8),
            wdec_bf, wdec_bf[:, :].rearrange("(dc q) o -> q dc o", q=128),
            tag="wdec")
        dma(p, "act", slab[1], slab[1].ap.rearrange("q (dc o) -> q dc o", dc=8),
            wgat_bf, wgat_bf[:, :].rearrange("(dc q) o -> q dc o", q=128),
            tag="wgat")
        for oc in range(8):
            for dc in range(8):
                mm(p, pmm, pmm.ap, slab[0],
                   slab[0][:, 1024 * dc + 128 * oc:1024 * dc + 128 * oc + 128],
                   h_fm, h_fm[:, 518 * dc + 6:518 * dc + 6 + TW],
                   start=(dc == 0), stop=(dc == 7), tag="decmm")
            act(p, dbuf, dbuf.ap, pmm, pmm.ap, AF.Sigmoid,
                bias=bdec_cols[:, oc:oc + 1], extra_reads=(bdec_cols,),
                tag="sigd")
            ts(p, "dve", dbuf, dbuf.ap, dbuf, dbuf.ap, 0.05, ALU.max,
               0.95, ALU.min, tag="clip")
            if it == 0 and oc == 0:
                dump("d0", dbuf, dbuf.ap, (128, 512))
            for dc in range(8):
                mm(p, pmms[1], pmms[1].ap, slab[1],
                   slab[1][:, 1024 * dc + 128 * oc:1024 * dc + 128 * oc + 128],
                   h_fm, h_fm[:, 518 * dc + 6:518 * dc + 6 + TW],
                   start=(dc == 0), stop=(dc == 7), tag="gatmm")
            act(p, gbuf2, gbuf2.ap, pmms[1], pmms[1].ap, AF.Sigmoid,
                bias=bgat_cols[:, oc:oc + 1], extra_reads=(bgat_cols,),
                tag="sigg")
            tt(p, "dve", updb, updb.ap, gbuf2, gbuf2.ap, h_fm,
               h_fm[:, 518 * oc + 6:518 * oc + 6 + TW], ALU.mult, tag="upd")
            copy(p, "dve", cumd, cumd[:, 0:1], carry_cd,
                 carry_cd[:, oc:oc + 1])
            scan(p, cumd, cumd[:, 1:TW + 1], dbuf, dbuf.ap, dbuf, dbuf.ap,
                 carry_cd[:, oc:oc + 1], ALU.mult, ALU.bypass,
                 extra_reads=(carry_cd,), tag="cumprod")
            copy(p, "dve", carry_cd, carry_cd[:, oc:oc + 1], cumd,
                 cumd[:, TW:TW + 1])
            ts(p, "dve", wrec, wrec.ap, cumd, cumd[:, 0:TW], 1e-7, ALU.max,
               tag="wmax")
            recip(p, wrec, wrec.ap, wrec, wrec.ap)
            tt(p, "dve", wrec, wrec.ap, wrec, wrec.ap, updb, updb.ap,
               ALU.mult, tag="wmul")
            scan(p, cum2, cum2.ap, wrec, wrec.ap, wrec, wrec.ap,
                 carry_c2[:, oc:oc + 1], ALU.add, ALU.bypass,
                 extra_reads=(carry_c2,), tag="cumsum")
            copy(p, "dve", carry_c2, carry_c2[:, oc:oc + 1], cum2,
                 cum2[:, TW - 1:TW])
            stt(p, "dve", state_fm, state_fm[:, TW * oc:TW * oc + TW],
                cum2, cum2.ap, ss_col, cumd, cumd[:, 1:TW + 1],
                ALU.mult, ALU.mult, extra_reads=(sc_cols,), tag="state")
        if it == 0:
            dump("state", state_fm, state_fm.ap, (128, 4096), BF16)
            dump("cd0", cumd, cumd.ap, (128, 513))
        for kc in range(2):
            for dc in range(8):
                mm(p, pmm, pmm.ap, wk_sb,
                   wk_sb[:, 256 * dc + 128 * kc:256 * dc + 128 * kc + 128],
                   h_fm, h_fm[:, 518 * dc + 6:518 * dc + 6 + TW],
                   start=(dc == 0), stop=(dc == 7), tag="kmm")
            copy(p, "dve", k_fm, k_fm[:, TW * kc:TW * kc + TW], pmm, pmm.ap)
        for jc in range(2):
            for ic in range(2):
                mm(p, pmm, pmm.ap, MT_sb,
                   MT_sb[:, 256 * ic + 128 * jc:256 * ic + 128 * jc + 128],
                   k_fm, k_fm[:, TW * ic:TW * ic + TW],
                   start=(ic == 0), stop=(ic == 1), tag="memomm")
            copy(p, "dve", memo_fm, memo_fm[:, TW * jc:TW * jc + TW],
                 pmm, pmm.ap)
        for oc in range(8):
            for jc in range(2):
                mm(p, pmm, pmm.ap, wmem_sb,
                   wmem_sb[:, 1024 * jc + 128 * oc:1024 * jc + 128 * oc + 128],
                   memo_fm, memo_fm[:, TW * jc:TW * jc + TW],
                   start=(jc == 0), stop=(jc == 1), tag="memmm")
            act(p, mem_fm, mem_fm[:, TW * oc:TW * oc + TW], pmm, pmm.ap,
                AF.Copy, scale=ms_col, extra_reads=(sc_cols,), tag="memsc")
        if it == 0:
            dump("mem", mem_fm, mem_fm.ap, (128, 4096), BF16)
            dump("kk", k_fm, k_fm.ap, (128, 1024), BF16)
        for dc in range(8):
            base = 518 * dc
            for j in range(7):
                ts(p, "dve", cscr, cscr[:, 512 * j:512 * j + 512],
                   h_fm, h_fm[:, base + j:base + j + TW],
                   conv_sb[:, 7 * dc + j:7 * dc + j + 1], ALU.mult,
                   extra_reads=(conv_sb,), tag="convp")
            tt(p, "pool", cscr, cscr[:, 512 * 0:512 * 0 + 512],
               cscr, cscr[:, 0:512], cscr, cscr[:, 512:1024], ALU.add,
               tag="conva")
            tt(p, "pool", cscr, cscr[:, 512 * 2:512 * 2 + 512],
               cscr, cscr[:, 1024:1536], cscr, cscr[:, 1536:2048], ALU.add,
               tag="conva")
            tt(p, "pool", cscr, cscr[:, 512 * 4:512 * 4 + 512],
               cscr, cscr[:, 2048:2560], cscr, cscr[:, 2560:3072], ALU.add,
               tag="conva")
            tt(p, "pool", cscr, cscr[:, 0:512],
               cscr, cscr[:, 0:512], cscr, cscr[:, 1024:1536], ALU.add,
               tag="conva")
            tt(p, "pool", cscr, cscr[:, 2048:2560],
               cscr, cscr[:, 2048:2560], cscr, cscr[:, 3072:3584], ALU.add,
               tag="conva")
            tt(p, "pool", local_fm, local_fm[:, TW * dc:TW * dc + TW],
               cscr, cscr[:, 0:512], cscr, cscr[:, 2048:2560], ALU.add,
               tag="conva")
        srcs = [local_fm, state_fm, mem_fm]
        for ph in range(2):
            for cc in range(24):
                rb4 = rch[(cc // 4) % 2]
                if cc % 4 == 0:
                    dma(p, "sp", rb4,
                        rb4.ap.rearrange("q (c o) -> q c o", c=4),
                        wc_bf, wc_bf[128 * cc:128 * cc + 512, :].rearrange(
                            "(c q) o -> q c o", q=128), tag="wc")
                rb = rb4
                for k2 in range(2):
                    ts_ = 2 * ph + k2
                    pwx = pw[k2]
                    if cc == 0:
                        mm(p, pwx, pwx.h[:, 0:512], cst["ones1"],
                           cst["ones1"].ap, bc_row, bc_row.h[0:1, 0:512],
                           start=True, stop=False, tag="bcb")
                        mm(p, pwx, pwx.h[:, 512:1024], cst["ones1"],
                           cst["ones1"].ap, bc_row, bc_row.h[0:1, 512:1024],
                           start=True, stop=False, tag="bcb")
                    src = srcs[cc // 8]
                    lap = src[:, TW * (cc % 8) + 128 * ts_:
                              TW * (cc % 8) + 128 * ts_ + 128]
                    for half in range(2):
                        mm(p, pwx, pwx.h[:, 512 * half:512 * half + 512],
                           src, lap, rb,
                           rb[:, 1024 * (cc % 4) + 512 * half:
                              1024 * (cc % 4) + 512 * half + 512],
                           start=False, stop=(cc == 23 and half == 1),
                           tag="wcmm")
            for k2 in range(2):
                ts_ = 2 * ph + k2
                xns = xn1_tm[:, 1024 * ts_:1024 * ts_ + 1024]
                tt(p, "dve", xn1_tm, xns, xn1_tm, xns, pw[k2], pw[k2].ap,
                   ALU.add, tag="mixed")
        if it == 0:
            dump("mixed", xn1_tm, xn1_tm.ap, (128, 4096))
        for ts_ in range(NT):
            xns = xn1_tm[:, 1024 * ts_:1024 * ts_ + 1024]
            act(p, junk1, junk1.ap, xn1_tm, xns, AF.Square,
                accum=stat1, accum_ap=col(L_SQ), tag="rms2sq")
            ts(p, "dve", stat1, col(L_SQ), stat1, col(L_SQ), 1.0 / D, ALU.mult)
            act(p, stat1, col(L_SQ), stat1, col(L_SQ), AF.Ln, bias=col(L_EPS2))
            act(p, stat1, col(L_SQ), stat1, col(L_SQ), AF.Exp, scale=-0.5)
            ts(p, "dve", junk1, junk1.ap, xn1_tm, xns, col(L_SQ), ALU.mult,
               extra_reads=(stat1,), tag="rms2apply")
            for dc in range(8):
                ptx = ptrs[dc % 2]
                transpose(p, ptx, ptx.h[:, 0:128], junk1,
                          junk1[:, 128 * dc:128 * dc + 128], cst["ident"])
                ts(p, "dve", h2_fm,
                   h2_fm[:, TW * dc + 128 * ts_:TW * dc + 128 * ts_ + 128],
                   ptx, ptx.h[:, 0:128], n2w_cols[:, dc:dc + 1], ALU.mult,
                   extra_reads=(n2w_cols,), tag="h2evac")
        for fq in range(4):
            for s2 in range(2):
                fs = 2 * fq + s2
                gs, us = gsl[s2], usl[s2]
                dma(p, "sp", gs, gs.ap.rearrange("q (dc f) -> q dc f", dc=8),
                    wgg_bf, wgg_bf[:, 512 * fs:512 * fs + 512].rearrange(
                        "(dc q) f -> q dc f", q=128), tag="wg")
                dma(p, "act", us, us.ap.rearrange("q (dc f) -> q dc f", dc=8),
                    wup_bf, wup_bf[:, 512 * fs:512 * fs + 512].rearrange(
                        "(dc q) f -> q dc f", q=128), tag="wu")
                for fl in range(4):
                    fi = 4 * s2 + fl
                    for dc in range(8):
                        mm(p, pmm, pmm.ap, gs,
                           gs[:, 512 * dc + 128 * fl:512 * dc + 128 * fl + 128],
                           h2_fm, h2_fm[:, TW * dc:TW * dc + TW],
                           start=(dc == 0), stop=(dc == 7), tag="gmm")
                    for dc in range(8):
                        mm(p, pmms[1], pmms[1].ap, us,
                           us[:, 512 * dc + 128 * fl:512 * dc + 128 * fl + 128],
                           h2_fm, h2_fm[:, TW * dc:TW * dc + TW],
                           start=(dc == 0), stop=(dc == 7), tag="umm")
                    sg = silg[fl % 2]
                    act(p, sg, sg.ap, pmm, pmm.ap, AF.Silu, tag="silu1")
                    tt(p, "dve", ffin, ffin[:, TW * fi:TW * fi + TW],
                       sg, sg.ap, pmms[1], pmms[1].ap, ALU.mult,
                       tag="ffinmul")
            for ph in range(2):
                for fi in range(8):
                    fc = 8 * fq + fi
                    rb4 = rch[(fi // 4) % 2]
                    if fi % 4 == 0:
                        dma(p, "sp", rb4,
                            rb4.ap.rearrange("q (c o) -> q c o", c=4),
                            wdn_bf,
                            wdn_bf[128 * fc:128 * fc + 512, :].rearrange(
                                "(c q) o -> q c o", q=128), tag="wdn")
                    for k2 in range(2):
                        ts_ = 2 * ph + k2
                        pwx = pw[k2]
                        for half in range(2):
                            mm(p, pwx, pwx.h[:, 512 * half:512 * half + 512],
                               ffin, ffin[:, TW * fi + 128 * ts_:
                                          TW * fi + 128 * ts_ + 128],
                               rb4, rb4[:, 1024 * (fi % 4) + 512 * half:
                                        1024 * (fi % 4) + 512 * half + 512],
                               start=(fi == 0), stop=(fi == 7 and half == 1),
                               tag="ffmm")
                for k2 in range(2):
                    ts_ = 2 * ph + k2
                    xns = xn1_tm[:, 1024 * ts_:1024 * ts_ + 1024]
                    tt(p, "dve", xn1_tm, xns, xn1_tm, xns, pw[k2],
                       pw[k2].ap, ALU.add, tag="ffadd")
        if it == 0:
            dump("h2", h2_fm, h2_fm.ap, (128, 4096), BF16)
            dump("smix", xn1_tm, xn1_tm.ap, (128, 4096))
        if it < NTW - 1:
            for dc in range(8):
                copy(p, "pool", h_fm, h_fm[:, 518 * dc:518 * dc + 6],
                     h_fm, h_fm[:, 518 * dc + TW:518 * dc + TW + 6],
                     tag="halo")
        for step in range(2):
            wsrc = ffw_bf if step == 0 else fa1_bf
            sl = slab[step]
            dma(p, "sp" if step == 0 else "act", sl,
                sl.ap.rearrange("q (dc o) -> q dc o", dc=8),
                wsrc, wsrc[:, :].rearrange("(dc q) o -> q dc o", q=128),
                tag="ffnw")
            for oc in range(8):
                for dc in range(8):
                    rap = (xn1_fm[:, TW * dc:TW * dc + TW] if step == 0 else
                           h_fm[:, 518 * dc + 6:518 * dc + 6 + TW])
                    rbuf = xn1_fm if step == 0 else h_fm
                    mm(p, pmm, pmm.ap, sl,
                       sl[:, 1024 * dc + 128 * oc:1024 * dc + 128 * oc + 128],
                       rbuf, rap, start=(dc == 0), stop=(dc == 7), tag="abmm")
                if step == 0:
                    copy(p, "dve", h_fm,
                         h_fm[:, 518 * oc + 6:518 * oc + 6 + TW], pmm, pmm.ap)
                else:
                    copy(p, "dve", local_fm,
                         local_fm[:, TW * oc:TW * oc + TW], pmm, pmm.ap)
        for ph in range(2):
            for cc in range(8):
                rb4 = rch[(cc // 4) % 2]
                if cc % 4 == 0:
                    dma(p, "sp", rb4,
                        rb4.ap.rearrange("q (c o) -> q c o", c=4),
                        fa2_bf, fa2_bf[128 * cc:128 * cc + 512, :].rearrange(
                            "(c q) o -> q c o", q=128), tag="fa2")
                rb = rb4
                for k2 in range(2):
                    ts_ = 2 * ph + k2
                    pwx = pw[k2]
                    if cc == 0:
                        for half in range(2):
                            mm(p, pwx, pwx.h[:, 512 * half:512 * half + 512],
                               cst["ones1"], cst["ones1"].ap, ffb_row,
                               ffb_row.h[0:1, 512 * half:512 * half + 512],
                               start=True, stop=False, tag="ffbb")
                    for half in range(2):
                        mm(p, pwx, pwx.h[:, 512 * half:512 * half + 512],
                           b_fm, b_fm[:, TW * cc + 128 * ts_:
                                      TW * cc + 128 * ts_ + 128],
                           rb, rb[:, 1024 * (cc % 4) + 512 * half:
                                  1024 * (cc % 4) + 512 * half + 512],
                           start=False, stop=(cc == 7 and half == 1),
                           tag="cmm")
            for k2 in range(2):
                ts_ = 2 * ph + k2
                i = 4 * it + ts_
                tbx, gbx, pbx, obx = tb[k2], gb[k2], tb[k2], gb[k2]
                act(p, tbx, tbx.ap, pw[k2], pw[k2].ap, AF.Tanh, tag="tanh")
                act(p, gbx, gbx.ap, tbx, tbx.ap, AF.Silu, tag="silu2")
                tt(p, "dve", pbx, pbx.ap, xn1_tm,
                   xn1_tm[:, 1024 * ts_:1024 * ts_ + 1024], gbx, gbx.ap,
                   ALU.mult, tag="sg")
                xt = x1t[k2]
                dma(p, "sp", xt, xt.ap, x1_in,
                    x1_in[128 * i:128 * i + 128, :], tag="x1load2")
                tt(p, "pool", obx, obx.ap, pbx, pbx.ap, xt, xt.ap, ALU.add,
                   tag="final")
                dma(p, "sp", out_d, out_d[128 * i:128 * i + 128, :], obx,
                    obx.ap, tag="outstore")


def make_l1_program(dbg=(), paranoid=False):
    nc = bass.Bass("TRN2", target_bir_lowering=False, debug=False)
    p = Prog(nc)
    cst = declare_constants(p)
    g = declare_l1_inputs(nc)
    x1 = Buf("x1", nc.dram_tensor("x1", [T, D], F32, kind="ExternalInput"))
    out = Buf("out", nc.dram_tensor("out", [T, D], F32, kind="ExternalOutput"))
    build_l1(p, g, cst, x1, out, dbg=dbg)
    p.emit(verbose=True, paranoid=paranoid)
    return nc


L1_RESHAPE = {"L1_attn_norm1_w": (8, 128), "L1_attn_norm2_w": (8, 128),
              "L1_attn_bdecay": (8, 128), "L1_attn_bgate": (8, 128),
              "L1_attn_state_scale": (1, 1), "L1_attn_mem_scale": (1, 1),
              "L1_attn_bc": (1, D), "L1_ffn_bias": (1, D)}


def l1_input_map(inputs, core, x1):
    c = host_constants()
    m = dict(c)
    m["x1"] = np.ascontiguousarray(x1)
    for k in inputs:
        if k.startswith("L1_"):
            v = np.ascontiguousarray(inputs[k])
            if k in L1_RESHAPE:
                v = v.reshape(L1_RESHAPE[k])
            m[k] = v
    return m



def make_fused_program(dbg=()):
    nc = bass.Bass("TRN2", target_bir_lowering=False, debug=False)
    p = Prog(nc)
    cst = declare_constants(p)
    g0 = declare_l0_inputs(nc)
    g1 = declare_l1_inputs(nc)
    x1 = p.dram("x1s", [T, D], F32, kind="Internal")
    out = Buf("out", nc.dram_tensor("out", [T, D], F32, kind="ExternalOutput"))
    p.push_scope()
    build_l0(p, g0, cst, x1, dbg=dbg)
    p.barrier()
    p.pop_scope()
    p.push_scope()
    build_l1(p, g1, cst, x1, out, dbg=dbg)
    p.pop_scope()
    p.emit(verbose=True)
    return nc


def fused_input_map(inputs, core):
    m = l0_input_map(inputs, core)
    for k in inputs:
        if k.startswith("L1_"):
            v = np.ascontiguousarray(inputs[k])
            if k in L1_RESHAPE:
                v = v.reshape(L1_RESHAPE[k])
            m[k] = v
    return m


def kernel(**inputs):
    inputs = {k: np.asarray(v) for k, v in inputs.items()}
    core_ids = list(range(NCORES))
    nc = make_fused_program()
    maps = [fused_input_map(inputs, c) for c in core_ids]
    res = run_bass_kernel_spmd(nc, maps, core_ids=core_ids)
    out = np.stack([res.results[c]["out"] for c in core_ids], axis=0)
    return out.astype(np.float32)
